# version 12
# speedup vs baseline: 1.0894x; 1.0300x over previous
import math
import numpy as np

import concourse.bass as bass
import concourse.mybir as mybir
from concourse.bass_utils import run_bass_kernel_spmd

F32 = mybir.dt.float32
BF16 = mybir.dt.bfloat16
AF = mybir.ActivationFunctionType
ALU = mybir.AluOpType
AX = mybir.AxisListType

B, T, D = 8, 2048, 1024
NCORES = 8
DT = D // 128
TC = T // 512
NEG = -30000.0


ENG = ("pe", "act", "dve", "sp", "gp")


class Prog:
    def __init__(self, nc: bass.Bass):
        self.nc = nc
        self.ops = []
        self.cnt = {e: 0 for e in ENG}
        self.stream_cnt = {}
        self.stream_queue = {}
        self.writer = {}
        self.readers = {}
        self._ccols = {}

    def ccol(self, value, dtype=F32):
        key = (float(value), dtype)
        if key not in self._ccols:
            t = self.nc.alloc_sbuf_tensor(f"cc_{len(self._ccols)}", [128, 1], dtype)
            z = ("cc", len(self._ccols))
            self.op("dve", lambda v: v.memset(t[:], float(value)), writes=(z,))
            self._ccols[key] = (t, z)
        t, z = self._ccols[key]
        return t[:], z

    def _record(self, tag, cnt, reads, writes):
        deps = {}

        def need(t, c):
            if deps.get(t, 0) < c:
                deps[t] = c

        for z in reads:
            w = self.writer.get(z)
            if w:
                need(*w)
        for z in writes:
            w = self.writer.get(z)
            if w:
                need(*w)
            for r in self.readers.get(z, ()):
                need(*r)
        for z in reads:
            self.readers.setdefault(z, []).append((tag, cnt))
        for z in writes:
            self.writer[z] = (tag, cnt)
            self.readers[z] = []
        return deps

    def op(self, eng, fn, reads=(), writes=()):
        self.cnt[eng] += 1
        deps = self._record("e:" + eng, self.cnt[eng], reads, writes)
        self.ops.append(dict(kind="op", eng=eng, fn=fn, deps=deps,
                             inc=("e:" + eng, self.cnt[eng])))

    def dma(self, stream, out, in_, reads=(), writes=(), queue=None, **kw):
        if stream not in self.stream_cnt:
            self.stream_cnt[stream] = 0
            self.stream_queue[stream] = queue or "sp"
        assert queue is None or self.stream_queue[stream] == queue, stream
        q = self.stream_queue[stream]
        self.stream_cnt[stream] += 1
        cnt = self.stream_cnt[stream] * 16
        deps = self._record("s:" + stream, cnt, reads, writes)
        if cnt > 16 and deps.get("s:" + stream, 0) < cnt - 16:
            deps["s:" + stream] = cnt - 16
        self.ops.append(dict(kind="dma", eng=q, out=out, in_=in_, kw=kw,
                             deps=deps, inc=("s:" + stream, cnt)))

    def emit(self):
        nc = self.nc
        from contextlib import ExitStack
        stack = ExitStack()
        sems = {}
        for e in ENG:
            sems["e:" + e] = stack.enter_context(nc.semaphore("sem_" + e))
        for s in self.stream_cnt:
            sems["s:" + s] = stack.enter_context(nc.semaphore("dma_" + s))
        block = stack.enter_context(nc.Block())

        per_eng = {e: [] for e in ENG}
        for o in self.ops:
            per_eng[o["eng"]].append(o)
        final_stream = {s: c * 16 for s, c in self.stream_cnt.items()}

        def emit_engine(builder, eng):
            known = {}

            def wait(t, c):
                if known.get(t, 0) >= c:
                    return
                builder.wait_ge(sems[t], c)
                known[t] = c

            for o in per_eng[eng]:
                for t, c in sorted(o["deps"].items()):
                    wait(t, c)
                if o["kind"] == "op":
                    inst = o["fn"](builder)
                    assert inst is not None, f"op fn returned None on {eng}"
                    t, c = o["inc"]
                    inst.then_inc(sems[t], 1)
                else:
                    inst = builder.dma_start(out=o["out"], in_=o["in_"], **o["kw"])
                    t, c = o["inc"]
                    inst.then_inc(sems[t], 16)
            if eng == "sp":
                for s, c in final_stream.items():
                    wait("s:" + s, c)

        secs = dict(pe="tensor", act="scalar", dve="vector", sp="sync", gp="gpsimd")
        for e in ENG:
            getattr(block, secs[e])(lambda b, _e=e: emit_engine(b, _e))
        stack.close()


class Psum:

    def __init__(self, nc):
        self.t = [nc.alloc_psum_tensor("psA", [128, 2048], F32),
                  nc.alloc_psum_tensor("psB", [128, 2048], F32)]

    def bank(self, i, rows=128, cols=512):
        assert cols <= 512 and 0 <= i < 8
        t = self.t[i // 4]
        off = (i % 4) * 512
        return t[0:rows, off:off + cols], ("psum", i)

    def span(self, i, nbanks, rows=128):
        assert (i % 4) + nbanks <= 4
        t = self.t[i // 4]
        off = (i % 4) * 512
        zones = tuple(("psum", i + j) for j in range(nbanks))
        return t[0:rows, off:off + nbanks * 512], zones


class Buf:
    def __init__(self, ap, zones, off=None, words=None, arname=None):
        self.ap = ap
        self.zones = zones
        self.off = off
        self.words = words
        self.arname = arname

    def __getitem__(self, idx):
        return self.ap[idx]

    @property
    def z(self):
        return list(self.zones)

    def zt(self, i, n):
        if self.off is None:
            return list(self.zones)
        w0 = self.off + (i * self.words) // n
        w1 = self.off + ((i + 1) * self.words + n - 1) // n
        return [(self.arname, b_) for b_ in range(w0 // 2048,
                                                  (w1 + 2047) // 2048)]


class Arena:

    def __init__(self, nc, kb=200, name="ar"):
        self.words = kb * 256
        self.base = nc.alloc_sbuf_tensor(name, [128, self.words], F32)
        self.off = 0
        self.name = name

    def alloc(self, shape, dtype):
        assert shape[0] == 128
        n = 1
        for s_ in shape[1:]:
            n *= s_
        words = n if dtype == F32 else (n + 1) // 2
        words = (words + 15) & ~15
        assert self.off + words <= self.words, \
            f"arena overflow: {(self.off + words) / 256:.0f}KB > {self.words // 256}KB"
        flat = self.base[:, self.off:self.off + words]
        if dtype != F32:
            flat = flat.bitcast(dtype)
        if len(shape) == 2:
            ap = flat[:, 0:shape[1]]
        elif len(shape) == 3:
            ap = flat[:, 0:shape[1] * shape[2]].rearrange(
                "p (a b) -> p a b", a=shape[1])
        else:
            raise ValueError(shape)
        blk0 = self.off // 2048
        blk1 = (self.off + words + 2047) // 2048
        zones = tuple((self.name, b_) for b_ in range(blk0, blk1))
        buf = Buf(ap, zones, off=self.off, words=words, arname=self.name)
        self.off += words
        return buf

    def mark(self):
        return self.off

    def reset(self, mark):
        self.off = mark


def mm_group(p: Prog, items, reads, wzones):
    def fn(t):
        last = None
        for (o, l, r, s0, s1) in items:
            last = t.matmul(o, l, r, start=s0, stop=s1)
        return last
    p.op("pe", fn, reads=tuple(reads), writes=tuple(wzones))


_LOAD_RR = [0]


def load_w(p, ar, dram, shape, stream=None):
    buf = ar.alloc(shape, BF16)
    if stream is None:
        _LOAD_RR[0] = (_LOAD_RR[0] + 1) % 3
        stream = f"wl{_LOAD_RR[0]}"
    p.dma(stream, buf.ap, dram, writes=buf.z, queue="gp")
    return buf


def load_f32(p, ar, dram, shape, stream="lload"):
    buf = ar.alloc(shape, F32)
    p.dma(stream, buf.ap, dram, writes=buf.z, queue="sp")
    return buf


def dram_ld(p, stream, buf, dram, dz=None, n_tiles=DT):
    for d in range(n_tiles):
        p.dma(stream, buf[:, d, :], dram[d * 128:(d + 1) * 128, :],
              reads=[("d", dz, d)] if dz else (),
              writes=buf.zt(d, n_tiles))


def dram_st(p, stream, dram, buf, dz=None, n_tiles=DT):
    for d in range(n_tiles):
        p.dma(stream, dram[d * 128:(d + 1) * 128, :], buf[:, d, :],
              reads=buf.zt(d, n_tiles),
              writes=[("d", dz, d)] if dz else ())



def phase_ln_core(p, ps, ar, nm, xsb, out_bf=None, out_f32=None,
                  gamma=None, beta=None, silu=False, eps=1e-5):
    m_scratch = ar.mark()
    x2 = out_bf if out_bf is not None else ar.alloc([128, DT, T], BF16)
    ones = ar.alloc([128, 128], F32)
    onesb = ar.alloc([128, 128], BF16)
    rstd = ar.alloc([128, T], F32)
    mrs = ar.alloc([128, T], F32)

    p.op("dve", lambda v: v.memset(ones.ap, 1.0), writes=ones.z)
    p.op("dve", lambda v: v.memset(onesb.ap, 1.0), writes=onesb.z)
    for d in range(DT):
        p.op("act", lambda a, d=d: a.activation(x2[:, d, :], xsb[:, d, :],
                                                AF.Square),
             reads=xsb.zt(d, DT), writes=x2.zt(d, DT))
    s1, zs1 = ps.span(0, 4)
    s2, zs2 = ps.span(4, 4)
    for c in range(TC):
        sl = slice(c * 512, (c + 1) * 512)
        mm_group(p, [(s1[:, sl], ones.ap, xsb[:, k, sl], k == 0, k == DT - 1)
                     for k in range(DT)],
                 reads=ones.z + xsb.z, wzones=(zs1[c],))
        mm_group(p, [(s2[:, sl], onesb.ap, x2[:, k, sl], k == 0, k == DT - 1)
                     for k in range(DT)],
                 reads=onesb.z + x2.z, wzones=(zs2[c],))
    p.op("dve", lambda v: v.tensor_scalar(mrs.ap, s1[:], 1.0 / D, None, ALU.mult),
         reads=list(zs1), writes=mrs.z)
    p.op("dve", lambda v: v.tensor_scalar(rstd.ap, s2[:], 1.0 / D, None, ALU.mult),
         reads=list(zs2), writes=rstd.z)
    p.op("dve", lambda v: v.tensor_tensor(s1[:], mrs.ap, mrs.ap, ALU.mult),
         reads=mrs.z, writes=list(zs1))
    p.op("dve", lambda v: v.tensor_tensor(rstd.ap, rstd.ap, s1[:], ALU.subtract),
         reads=rstd.z + list(zs1), writes=rstd.z)
    epsc, zeps = p.ccol(eps)
    p.op("act", lambda a: a.activation(rstd.ap, rstd.ap, AF.Ln, bias=epsc),
         reads=rstd.z + [zeps], writes=rstd.z)
    p.op("act", lambda a: a.activation(rstd.ap, rstd.ap, AF.Exp, scale=-0.5),
         reads=rstd.z, writes=rstd.z)
    p.op("dve", lambda v: v.tensor_tensor(mrs.ap, mrs.ap, rstd.ap, ALU.mult),
         reads=mrs.z + rstd.z, writes=mrs.z)
    p.op("dve", lambda v: v.tensor_scalar(mrs.ap, mrs.ap, -1.0, None, ALU.mult),
         reads=mrs.z, writes=mrs.z)
    fn = AF.Silu if silu else AF.Identity
    for d in range(DT):
        p.op("dve", lambda v, d=d: v.tensor_tensor(s2[:], xsb[:, d, :], rstd.ap,
                                                   ALU.mult),
             reads=xsb.zt(d, DT) + rstd.z, writes=list(zs2))
        p.op("dve", lambda v, d=d: v.tensor_tensor(s1[:], s2[:], mrs.ap,
                                                   ALU.add),
             reads=list(zs2) + mrs.z, writes=list(zs1))
        if gamma is not None:
            if out_bf is not None:
                p.op("act", lambda a, d=d: a.activation(
                    out_bf[:, d, :], s1[:], fn, bias=beta[:, d:d + 1],
                    scale=gamma[:, d:d + 1]),
                    reads=list(zs1) + gamma.z + beta.z, writes=out_bf.zt(d, DT))
            if out_f32 is not None:
                p.op("act", lambda a, d=d: a.activation(
                    out_f32[:, d, :], s1[:], fn, bias=beta[:, d:d + 1],
                    scale=gamma[:, d:d + 1]),
                    reads=list(zs1) + gamma.z + beta.z, writes=out_f32.zt(d, DT))
        else:
            if out_bf is not None:
                p.op("dve", lambda v, d=d: v.tensor_copy(out_bf[:, d, :], s1[:]),
                     reads=list(zs1), writes=out_bf.zt(d, DT))
            if out_f32 is not None:
                p.op("dve", lambda v, d=d: v.tensor_copy(out_f32[:, d, :], s1[:]),
                     reads=list(zs1), writes=out_f32.zt(d, DT))
    ar.reset(m_scratch)


def phase_ln(p, ps, ar, nm, x_dram, want_f32=False, eps=1e-5, xdz="x"):
    xn = ar.alloc([128, DT, T], BF16)
    xnf = ar.alloc([128, DT, T], F32) if want_f32 else None
    m_scratch = ar.mark()
    xsb = ar.alloc([128, DT, T], F32)
    dram_ld(p, nm + "ldx", xsb, x_dram, dz=xdz)
    phase_ln_core(p, ps, ar, nm, xsb, out_bf=xn, out_f32=xnf, eps=eps)
    ar.reset(m_scratch)
    return xn, xnf


def proj(p, ps, w, x, out, dout_tiles, evac, evac_reads=(), bank0=0,
         kt=DT, n=T, krows=128):
    nch = n // 512
    for m in range(dout_tiles):
        for c in range(nch):
            bk, zbk = ps.bank(bank0 + (m * nch + c) % 2)
            sl = slice(c * 512, (c + 1) * 512)
            mm_group(p, [(bk, w.ap[0:krows, k, m * 128:(m + 1) * 128],
                          x.ap[0:krows, k, sl], k == 0, k == kt - 1)
                         for k in range(kt)],
                     reads=w.z + x.z, wzones=(zbk,))
            p.op("act", lambda a, bk=bk, m=m, sl=sl: evac(a, bk, out[:, m, sl], m),
                 reads=[zbk] + list(evac_reads), writes=out.z)


def phase_residual(p, ps, ar, nm, x_dram, s_dram, g_dram):
    m0 = ar.mark()
    xb = ar.alloc([128, DT, T], F32)
    sb = ar.alloc([128, DT, T], BF16)
    gb = ar.alloc([128, DT, T], BF16)
    dram_ld(p, nm + "rx", xb, x_dram, dz="x")
    dram_ld(p, nm + "rs", sb, s_dram, dz=nm + "s")
    dram_ld(p, nm + "rg", gb, g_dram, dz=nm + "g")
    sc, zsc = ps.span(0, 4)
    for d in range(DT):
        p.op("dve", lambda v, d=d: v.tensor_tensor(sc[:, 0:T], sb[:, d, :],
                                                   gb[:, d, :], ALU.mult),
             reads=sb.zt(d, DT) + gb.zt(d, DT), writes=list(zsc))
        p.op("dve", lambda v, d=d: v.tensor_tensor(xb[:, d, :], xb[:, d, :],
                                                   sc[:, 0:T], ALU.add),
             reads=xb.zt(d, DT) + list(zsc), writes=xb.zt(d, DT))
        p.dma(nm + "wx", x_dram[d * 128:(d + 1) * 128, :], xb[:, d, :],
              reads=xb.zt(d, DT), writes=[("d", "x", d)])
    ar.reset(m0)



L1C = 64
L1K = T // L1C


def layer1(p, ps, ar, w, x_dram, s_dram, g_dram):
    m0 = ar.mark()
    xn, _ = phase_ln(p, ps, ar, "l1n", x_dram)

    mf = ar.mark()
    win = load_w(p, ar, w["L1_ffn_Win"], [128, DT, 64])
    rep = ar.alloc([128, DT, 128], BF16)
    p.dma("wload", rep[0:64, :, :], w["l1_REP"], writes=rep.z, queue="gp")
    s64 = ar.alloc([128, T], BF16)
    h1 = ar.alloc([128, DT, T], BF16)
    w1c = load_f32(p, ar, w["l1_w1col"], [128, DT])
    b1c = load_f32(p, ar, w["l1_b1col"], [128, DT])
    w2c = load_f32(p, ar, w["l1_w2col"], [128, DT])
    maskc = load_w(p, ar, w["l1_MASK"], [128, DT, 64])
    bd = ar.alloc([128, DT, 64], BF16)
    g64 = ar.alloc([128, T], BF16)
    wout = ar.alloc([128, DT, 128], BF16)
    p.dma("wload", wout[0:64, :, :], w["L1_ffn_Wout"], writes=wout.z, queue="gp")
    gout = ar.alloc([128, DT, T], BF16)
    binc = ar.alloc([128, 1], F32)
    b2c = ar.alloc([128, 1], F32)
    p.dma("lload", binc[0:64, :], w["l1_bincol"], writes=binc.z, queue="sp")
    p.dma("lload", b2c[0:64, :], w["l1_b2col"], writes=b2c.z, queue="sp")
    boutc = load_f32(p, ar, w["l1_boutcol"], [128, DT])

    for c in range(TC):
        sl = slice(c * 512, (c + 1) * 512)
        bk, zbk = ps.bank(c % 2, rows=64)
        mm_group(p, [(bk, win[:, k, :], xn[:, k, sl], k == 0, k == DT - 1)
                     for k in range(DT)],
                 reads=win.z + xn.z, wzones=(zbk,))
        p.op("act", lambda a, bk=bk, sl=sl: a.activation(
            s64[0:64, sl], bk, AF.Identity, bias=binc[0:64, :]),
            reads=[zbk] + binc.z, writes=s64.z)
    for m in range(DT):
        for c in range(TC):
            sl = slice(c * 512, (c + 1) * 512)
            bk, zbk = ps.bank(2 + (m * TC + c) % 2)
            mm_group(p, [(bk, rep[0:64, m, :], s64[0:64, sl], True, True)],
                     reads=rep.z + s64.z, wzones=(zbk,))
            p.op("act", lambda a, bk=bk, m=m, sl=sl: a.activation(
                h1[:, m, sl], bk, AF.Relu, bias=b1c[:, m:m + 1],
                scale=w1c[:, m:m + 1]),
                reads=[zbk] + b1c.z + w1c.z, writes=h1.z)
    for k in range(DT):
        p.op("dve", lambda v, k=k: v.tensor_scalar(
            bd[:, k, :], maskc[:, k, :], w2c[:, k:k + 1], None, ALU.mult),
            reads=maskc.z + w2c.z, writes=bd.z)
    for c in range(TC):
        sl = slice(c * 512, (c + 1) * 512)
        bk, zbk = ps.bank(4 + c % 2, rows=64)
        mm_group(p, [(bk, bd[:, k, :], h1[:, k, sl], k == 0, k == DT - 1)
                     for k in range(DT)],
                 reads=bd.z + h1.z, wzones=(zbk,))
        p.op("act", lambda a, bk=bk, sl=sl: a.activation(
            g64[0:64, sl], bk, AF.Identity, bias=b2c[0:64, :]),
            reads=[zbk] + b2c.z, writes=g64.z)
    for m in range(DT):
        for c in range(TC):
            sl = slice(c * 512, (c + 1) * 512)
            bk, zbk = ps.bank(6 + (m * TC + c) % 2)
            mm_group(p, [(bk, wout[0:64, m, :], g64[0:64, sl], True, True)],
                     reads=wout.z + g64.z, wzones=(zbk,))
            p.op("act", lambda a, bk=bk, m=m, sl=sl: a.activation(
                gout[:, m, sl], bk, AF.Silu, bias=boutc[:, m:m + 1]),
                reads=[zbk] + boutc.z, writes=gout.z)
    dram_st(p, "l1wg", g_dram, gout, dz="l1rg")
    ar.reset(mf)

    uB = ar.alloc([128, DT, T], BF16)
    ssb = ar.alloc([128, DT, T], BF16)
    u = ar.alloc([128, DT, T], BF16)
    ms = ar.mark()

    wu = load_w(p, ar, w["l1_WuT"], [128, DT, D])
    wg = load_w(p, ar, w["l1_WgT"], [128, DT, D])
    buc = load_f32(p, ar, w["l1_bucol"], [128, DT])
    bgc = load_f32(p, ar, w["l1_bgcol"], [128, DT])
    upre = ar.alloc([128, DT, T], BF16)

    def ev_gelu(a, bk, o, m):
        return a.activation(o, bk, AF.Gelu, bias=buc[:, m:m + 1])
    proj(p, ps, wu, xn, upre, DT, ev_gelu, buc.z, bank0=0)

    def ev_sig(a, bk, o, m):
        return a.activation(o, bk, AF.Sigmoid, bias=bgc[:, m:m + 1])
    proj(p, ps, wg, xn, u, DT, ev_sig, bgc.z, bank0=2)
    for d in range(DT):
        p.op("dve", lambda v, d=d: v.tensor_tensor(u[:, d, :], upre[:, d, :],
                                                   u[:, d, :], ALU.mult),
             reads=upre.zt(d, DT) + u.zt(d, DT), writes=u.zt(d, DT))
    ar.reset(ms)

    bm = load_w(p, ar, w["l1_BmT"], [128, DT, D])

    def ev_cp(a, bk, o, m):
        return a.activation(o, bk, AF.Identity)
    proj(p, ps, bm, u, uB, DT, ev_cp, bank0=0)
    ar.reset(ms)

    at = load_w(p, ar, w["l1_AT"], [128, DT, D])
    sl_f = ar.alloc([128, DT, L1K], F32)
    sblo = ar.alloc([128, DT, L1K], BF16)
    bbuf = ar.alloc([128, DT, L1K], F32)
    ident = load_w(p, ar, w["identity"], [128, 128], stream="wload")
    ZS = ("l1", "ss")

    p.op("dve", lambda v: v.tensor_copy(sl_f.ap, uB.ap[:, :, 0::L1C]),
         reads=uB.z, writes=[ZS] + sl_f.z)
    p.op("dve", lambda v: v.tensor_copy(ssb.ap[:, :, 0::L1C], sl_f.ap),
         reads=[ZS], writes=ssb.z)
    p.op("dve", lambda v: v.tensor_copy(sblo.ap, sl_f.ap),
         reads=[ZS], writes=sblo.z)

    def scan_sweep(jrange, store_local, add_into_ssb):
        for j in jrange:
            spsum, zsp = ps.span(0, 1)
            sp3 = spsum.rearrange("p (a b) -> p a b", a=DT)[:, :, 0:L1K]
            items = []
            for jt in range(DT):
                for it in range(DT):
                    items.append((spsum[:, jt * 64:jt * 64 + L1K],
                                  at[:, it, jt * 128:(jt + 1) * 128],
                                  sblo[:, it, :], it == 0, it == DT - 1))
            mm_group(p, items, reads=at.z + sblo.z, wzones=(zsp[0],))
            if store_local:
                p.op("dve", lambda v, sp3=sp3, j=j: v.tensor_tensor(
                    sl_f.ap, sp3, uB.ap[:, :, j::L1C], ALU.add),
                    reads=[zsp[0]] + uB.z, writes=[ZS] + sl_f.z)
                p.op("dve", lambda v: v.tensor_copy(sblo.ap, sl_f.ap),
                     reads=[ZS], writes=sblo.z)
                p.op("dve", lambda v, j=j: v.tensor_copy(
                    ssb.ap[:, :, j::L1C], sl_f.ap),
                    reads=[ZS], writes=ssb.z)
            else:
                p.op("dve", lambda v, sp3=sp3: v.tensor_copy(sblo.ap, sp3),
                     reads=[zsp[0]], writes=sblo.z)
                p.op("dve", lambda v, j=j, sp3=sp3: v.tensor_tensor(
                    ssb.ap[:, :, j::L1C], ssb.ap[:, :, j::L1C], sp3, ALU.add),
                    reads=ssb.z + [zsp[0]], writes=ssb.z)

    scan_sweep(range(1, L1C), store_local=True, add_into_ssb=False)

    pw0 = load_w(p, ar, w["l1_AT"], [128, DT, D], stream="wload")
    ptr = ar.alloc([128, DT, D], BF16)
    pw1 = ar.alloc([128, DT, D], BF16)
    pws = [pw0, pw1]
    cur = 0
    for sq in range(6):
        src, dst = pws[cur], pws[1 - cur]
        for a_ in range(DT):
            for b_ in range(DT):
                bk, zbk = ps.bank(2)
                bkb = bk.bitcast(BF16)[:, 0:128]
                p.op("pe", lambda t, a_=a_, b_=b_, bkb=bkb, src=src: t.transpose(
                    bkb, src[:, a_, b_ * 128:(b_ + 1) * 128], ident.ap),
                    reads=src.z + ident.z, writes=(zbk,))
                p.op("dve", lambda v, a_=a_, b_=b_, bkb=bkb: v.tensor_copy(
                    ptr[:, b_, a_ * 128:(a_ + 1) * 128], bkb),
                    reads=[zbk], writes=ptr.z)
        for m in range(DT):
            for c in range(2):
                sl = slice(c * 512, (c + 1) * 512)
                bk, zbk = ps.bank(3 + (m * 2 + c) % 2)
                mm_group(p, [(bk, ptr[:, k, m * 128:(m + 1) * 128],
                              src[:, k, sl], k == 0, k == DT - 1)
                             for k in range(DT)],
                         reads=ptr.z + src.z, wzones=(zbk,))
                p.op("act", lambda a, bk=bk, m=m, sl=sl, dst=dst: a.activation(
                    dst[:, m, sl], bk, AF.Identity),
                    reads=[zbk], writes=dst.z)
        cur = 1 - cur
    p64 = pws[cur]

    bcol = ar.alloc([128, DT], F32)
    bcolb = ar.alloc([128, DT], BF16)
    ZB = ("l1", "bcol")
    p.op("dve", lambda v: v.memset(bcol.ap, 0.0), writes=[ZB] + bcol.z)
    p.op("dve", lambda v: v.memset(bbuf.ap, 0.0), writes=bbuf.z)
    for k in range(1, L1K):
        p.op("dve", lambda v: v.tensor_copy(bcolb.ap, bcol.ap),
             reads=[ZB], writes=bcolb.z)
        bk8, zb8 = ps.bank(2, cols=DT)
        items = []
        for jt in range(DT):
            for it in range(DT):
                items.append((bk8[:, jt:jt + 1],
                              p64[:, it, jt * 128:(jt + 1) * 128],
                              bcolb[:, it:it + 1], it == 0, it == DT - 1))
        mm_group(p, items, reads=p64.z + bcolb.z, wzones=(zb8,))
        p.op("dve", lambda v, k=k: v.tensor_tensor(
            bcol.ap, bk8, sl_f.ap[:, :, k - 1], ALU.add),
            reads=[zb8, ZS] + sl_f.z, writes=[ZB] + bcol.z)
        p.op("dve", lambda v, k=k: v.tensor_copy(bbuf[:, :, k], bcol.ap),
             reads=[ZB], writes=bbuf.z)

    p.op("dve", lambda v: v.tensor_copy(sblo.ap, bbuf.ap),
         reads=bbuf.z, writes=sblo.z)
    scan_sweep(range(0, L1C), store_local=False, add_into_ssb=True)
    ar.reset(ms)

    cm = load_w(p, ar, w["l1_CmT"], [128, DT, D])
    dcol = load_f32(p, ar, w["l1_Dcol"], [128, DT])
    yout = ar.alloc([128, DT, T], BF16)

    def ev_d(a, bk, o, m):
        return a.activation(o, bk, AF.Identity, bias=dcol[:, m:m + 1])
    proj(p, ps, cm, ssb, yout, DT, ev_d, dcol.z, bank0=0)
    dram_st(p, "l1ws", s_dram, yout, dz="l1rs")
    ar.reset(m0)

    phase_residual(p, ps, ar, "l1r", x_dram, s_dram, g_dram)



def lhsT_host(Wdindout):
    din, dout = Wdindout.shape
    assert din % 128 == 0
    return np.ascontiguousarray(
        Wdindout.reshape(din // 128, 128, dout).transpose(1, 0, 2))


def col_host(bvec):
    n = bvec.shape[0]
    assert n % 128 == 0
    return np.ascontiguousarray(bvec.reshape(n // 128, 128).T)


def prep_l1(inp):
    w = {}
    w["l1_WuT"] = lhsT_host(np.ascontiguousarray(inp["L1_attn_W_u"]).T)
    w["l1_WgT"] = lhsT_host(np.ascontiguousarray(inp["L1_attn_W_g"]).T)
    w["l1_BmT"] = lhsT_host(np.ascontiguousarray(inp["L1_attn_Bm"]).T)
    w["l1_AT"] = lhsT_host(np.ascontiguousarray(inp["L1_attn_A"]).T)
    w["l1_CmT"] = lhsT_host(np.ascontiguousarray(inp["L1_attn_Cm"]).T)
    w["l1_bucol"] = col_host(inp["L1_attn_b_u"])
    w["l1_bgcol"] = col_host(inp["L1_attn_b_g"])
    w["l1_Dcol"] = col_host(inp["L1_attn_D"])
    w["L1_ffn_Win"] = lhsT_host(np.asarray(inp["L1_ffn_Win"]))
    w["l1_bincol"] = np.ascontiguousarray(np.asarray(inp["L1_ffn_bin_"]).reshape(64, 1))
    REP = np.zeros((64, 1024), np.float32)
    for m_ in range(64):
        REP[m_, m_ * 16:(m_ + 1) * 16] = 1.0
    w["l1_REP"] = np.ascontiguousarray(REP.reshape(64, 8, 128))
    w["l1_w1col"] = col_host(np.asarray(inp["L1_ffn_W1"]).reshape(-1))
    w["l1_b1col"] = col_host(np.asarray(inp["L1_ffn_b1"]).reshape(-1))
    w["l1_w2col"] = col_host(np.asarray(inp["L1_ffn_W2"]).reshape(-1))
    MASK = np.zeros((1024, 64), np.float32)
    for m_ in range(64):
        MASK[m_ * 16:(m_ + 1) * 16, m_] = 1.0
    w["l1_MASK"] = lhsT_host(MASK)
    w["l1_b2col"] = np.ascontiguousarray(np.asarray(inp["L1_ffn_b2"]).reshape(64, 1))
    w["L1_ffn_Wout"] = np.ascontiguousarray(
        np.asarray(inp["L1_ffn_Wout"]).reshape(64, 8, 128))
    w["l1_boutcol"] = col_host(np.asarray(inp["L1_ffn_bout"]))
    w["identity"] = np.eye(128, dtype=np.float32)
    return w


L1_WSHAPES = dict(
    l1_WuT=[128, DT, D], l1_WgT=[128, DT, D], l1_BmT=[128, DT, D],
    l1_AT=[128, DT, D], l1_CmT=[128, DT, D],
    l1_bucol=[128, DT], l1_bgcol=[128, DT], l1_Dcol=[128, DT],
    L1_ffn_Win=[128, DT, 64], l1_bincol=[64, 1], l1_REP=[64, 8, 128],
    l1_w1col=[128, DT], l1_b1col=[128, DT], l1_w2col=[128, DT],
    l1_MASK=[128, DT, 64], l1_b2col=[64, 1],
    L1_ffn_Wout=[64, 8, 128], l1_boutcol=[128, DT],
    identity=[128, 128],
)



def build_layer1_test():
    nc = bass.Bass("TRN2", target_bir_lowering=False, debug=False)
    x = nc.dram_tensor("x", [D, T], F32, kind="ExternalInput").ap()
    w = {k: nc.dram_tensor(k, shp, F32, kind="ExternalInput").ap()
         for k, shp in L1_WSHAPES.items()}
    xout = nc.dram_tensor("xout", [D, T], F32, kind="ExternalOutput").ap()
    s_d = nc.dram_tensor("s_scr", [D, T], BF16, kind="ExternalOutput").ap()
    g_d = nc.dram_tensor("g_scr", [D, T], BF16, kind="ExternalOutput").ap()
    xi = nc.dram_tensor("x_int", [D, T], F32, kind="Internal").ap()
    p = Prog(nc)
    ps = Psum(nc)
    ar = Arena(nc, kb=202)
    tmp = ar.alloc([128, DT, T], F32)
    for d in range(DT):
        p.dma("xc", tmp[:, d, :], x[d * 128:(d + 1) * 128, :],
              writes=tmp.zt(d, DT))
        p.dma("xc2", xi[d * 128:(d + 1) * 128, :], tmp[:, d, :],
              reads=tmp.zt(d, DT), writes=[("d", "x", d)])
    ar.reset(0)
    layer1(p, ps, ar, w, xi, s_d, g_d)
    tmp2 = ar.alloc([128, DT, T], F32)
    for d in range(DT):
        p.dma("xo", tmp2[:, d, :], xi[d * 128:(d + 1) * 128, :],
              reads=[("d", "x", d)], writes=tmp2.zt(d, DT))
        p.dma("xo2", xout[d * 128:(d + 1) * 128, :], tmp2[:, d, :],
              reads=tmp2.zt(d, DT))
    p.emit()
    return nc



def layer2(p, ps, ar, w, x_dram, s_dram, g_dram, zmoe_dram, wrt_dram):
    m0 = ar.mark()
    xn, _ = phase_ln(p, ps, ar, "l2n", x_dram)

    mf = ar.mark()
    gwp = load_w(p, ar, w["l2_gatewT"], [128, DT, 128])
    lg = ar.alloc([128, T], BF16)
    ident = load_w(p, ar, w["identity"], [128, 128])
    wtok = ar.alloc([128, 16, 8], BF16)
    wT = ar.alloc([128, T], BF16)
    rcol = ar.alloc([128, 6], F32)
    rscr = ar.alloc([128, 16], BF16)
    for c in range(TC):
        sl = slice(c * 512, (c + 1) * 512)
        bk, zbk = ps.bank(c % 2)
        mm_group(p, [(bk, gwp[:, k, :], xn[:, k, sl], k == 0, k == DT - 1)
                     for k in range(DT)],
                 reads=gwp.z + xn.z, wzones=(zbk,))
        p.op("act", lambda a, bk=bk, sl=sl: a.activation(lg[:, sl], bk,
                                                         AF.Identity),
             reads=[zbk], writes=lg.z)
    for tt in range(16):
        sl = slice(tt * 128, (tt + 1) * 128)
        bk, zbk = ps.bank(2)
        bkv = bk.bitcast(BF16)[:, 0:8]
        p.op("pe", lambda t, sl=sl, bkv=bkv: t.transpose(
            bkv, lg[0:8, sl], ident[0:8, 0:8]),
            reads=lg.z + ident.z, writes=(zbk,))
        p.op("dve", lambda v, tt=tt, bkv=bkv: v.tensor_copy(wtok[:, tt, :], bkv),
             reads=[zbk], writes=wtok.z)
        p.op("dve", lambda v, tt=tt: v.reduce_max(rcol[:, 0:1], wtok[:, tt, :],
                                                  axis=AX.X, negate=True),
             reads=wtok.z, writes=rcol.z)
        p.op("act", lambda a, tt=tt: a.activation(
            wtok[:, tt, :], wtok[:, tt, :], AF.Exp, bias=rcol[:, 0:1],
            accum_out=rcol[:, 1:2]),
            reads=wtok.z + rcol.z, writes=wtok.z + rcol.z)
        p.op("dve", lambda v: v.reciprocal(rcol[:, 2:3], rcol[:, 1:2]),
             reads=rcol.z, writes=rcol.z)
        p.op("dve", lambda v, tt=tt: v.tensor_scalar(
            wtok[:, tt, :], wtok[:, tt, :], rcol[:, 2:3], None, ALU.mult),
            reads=wtok.z + rcol.z, writes=wtok.z)
        p.op("dve", lambda v, tt=tt: v.reduce_max(rcol[:, 3:4], wtok[:, tt, :],
                                                  axis=AX.X),
             reads=wtok.z, writes=rcol.z)
        p.op("dve", lambda v, tt=tt: v.tensor_scalar(
            rscr[:, 0:8], wtok[:, tt, :], rcol[:, 3:4], None, ALU.is_equal),
            reads=wtok.z + rcol.z, writes=rscr.z)
        p.op("dve", lambda v, tt=tt: v.scalar_tensor_tensor(
            rscr[:, 8:16], rscr[:, 0:8], NEG, wtok[:, tt, :], ALU.mult, ALU.add),
            reads=rscr.z + wtok.z, writes=rscr.z)
        p.op("dve", lambda v: v.reduce_max(rcol[:, 4:5], rscr[:, 8:16],
                                           axis=AX.X),
             reads=rscr.z, writes=rcol.z)
        p.op("dve", lambda v, tt=tt: v.tensor_scalar(
            rscr[:, 0:8], wtok[:, tt, :], rcol[:, 4:5], None, ALU.is_ge),
            reads=wtok.z + rcol.z, writes=rscr.z)
        p.op("dve", lambda v, tt=tt: v.tensor_tensor(
            wtok[:, tt, :], wtok[:, tt, :], rscr[:, 0:8], ALU.mult),
            reads=wtok.z + rscr.z, writes=wtok.z)
        bk3, zbk3 = ps.bank(4)
        bk3v = bk3.bitcast(BF16)[0:8, 0:128]
        p.op("pe", lambda t, tt=tt, bk3v=bk3v: t.transpose(
            bk3v, wtok[:, tt, :], ident.ap),
            reads=wtok.z + ident.z, writes=(zbk3,))
        p.op("dve", lambda v, sl=sl, bk3v=bk3v: v.tensor_copy(wT[0:8, sl],
                                                              bk3v),
             reads=[zbk3], writes=wT.z)

    p.dma("l2wrt", wrt_dram, wT[0:8, :], reads=wT.z, writes=[("d", "l2wrt")])
    me = ar.mark()
    w1b = ar.alloc([128, DT, 1536], BF16)
    w2b = ar.alloc([128, 6, D], BF16)
    actw = ar.alloc([128, 6, 512], BF16)
    sgb = ar.alloc([128, 6, 512], BF16)
    zc = ar.alloc([128, DT, 512], F32)
    wbc = ar.alloc([128, 512], BF16)
    for c in range(TC):
        sl = slice(c * 512, (c + 1) * 512)
        p.op("dve", lambda v: v.memset(zc.ap, 0.0), writes=zc.z)
        for e in range(8):
            p.dma("l2w1", w1b.ap, w["l2_W1T"][e], writes=w1b.z, queue="gp")
            p.dma("l2w2", w2b.ap, w["l2_W2T"][e], writes=w2b.z, queue="gp")
            p.dma("l2wbc", wbc.ap,
                  wrt_dram[e:e + 1, sl].to_broadcast([128, 512]),
                  reads=[("d", "l2wrt")], writes=wbc.z)
            for m in range(12):
                bk, zbk = ps.bank((m % 2))
                mm_group(p, [(bk, w1b[:, k, m * 128:(m + 1) * 128],
                              xn[:, k, sl], k == 0, k == DT - 1)
                             for k in range(DT)],
                         reads=w1b.z + xn.z, wzones=(zbk,))
                if m < 6:
                    p.op("dve", lambda v, bk=bk, m=m: v.tensor_tensor(
                        actw[:, m, :], bk, wbc.ap, ALU.mult),
                        reads=[zbk] + wbc.z, writes=actw.z)
                else:
                    p.op("act", lambda a, bk=bk, m=m: a.activation(
                        sgb[:, m - 6, :], bk, AF.Silu),
                        reads=[zbk], writes=sgb.z)
            for m in range(6):
                p.op("dve", lambda v, m=m: v.tensor_tensor(
                    actw[:, m, :], actw[:, m, :], sgb[:, m, :], ALU.mult),
                    reads=actw.z + sgb.z, writes=actw.z)
            for m in range(DT):
                bk, zbk = ps.bank(2 + m % 2)
                mm_group(p, [(bk, w2b[:, k, m * 128:(m + 1) * 128][0:128],
                              actw[:, k, :], k == 0, k == 5)
                             for k in range(6)],
                         reads=w2b.z + actw.z, wzones=(zbk,))
                p.op("dve", lambda v, bk=bk, m=m: v.tensor_tensor(
                    zc[:, m, :], zc[:, m, :], bk, ALU.add),
                    reads=zc.z + [zbk], writes=zc.z)
        for m in range(DT):
            p.dma("l2zst", zmoe_dram[m * 128:(m + 1) * 128, sl], zc[:, m, :],
                  reads=zc.z, writes=[("d", "zmoe", m, c)])
    ar.reset(me)

    ws1 = load_w(p, ar, w["l2_Ws1T"], [128, DT, 3072])
    act1 = ar.alloc([128, 12, T], BF16)
    for m in range(12):
        for c in range(TC):
            sl = slice(c * 512, (c + 1) * 512)
            bky, zbky = ps.bank(m % 2)
            bkg, zbkg = ps.bank(2 + m % 2)
            mm_group(p, [(bky, ws1[:, k, m * 128:(m + 1) * 128],
                          xn[:, k, sl], k == 0, k == DT - 1)
                         for k in range(DT)],
                     reads=ws1.z + xn.z, wzones=(zbky,))
            mm_group(p, [(bkg, ws1[:, k, 1536 + m * 128:1536 + (m + 1) * 128],
                          xn[:, k, sl], k == 0, k == DT - 1)
                         for k in range(DT)],
                     reads=ws1.z + xn.z, wzones=(zbkg,))
            p.op("act", lambda a, bkg=bkg, m=m, sl=sl: a.activation(
                act1[:, m, sl], bkg, AF.Silu),
                reads=[zbkg], writes=act1.z)
            p.op("dve", lambda v, bky=bky, m=m, sl=sl: v.tensor_tensor(
                act1[:, m, sl], act1[:, m, sl], bky, ALU.mult),
                reads=act1.z + [zbky], writes=act1.z)
    ws2 = load_w(p, ar, w["l2_Ws2T"], [128, 12, D])
    zcl = ar.alloc([128, 512], F32)
    gout = ar.alloc([128, DT, T], BF16)
    for m in range(DT):
        for c in range(TC):
            sl = slice(c * 512, (c + 1) * 512)
            bk, zbk = ps.bank(4 + (m + c) % 2)
            mm_group(p, [(bk, ws2[:, k, m * 128:(m + 1) * 128],
                          act1[:, k, sl], k == 0, k == 11)
                         for k in range(12)],
                     reads=ws2.z + act1.z, wzones=(zbk,))
            p.dma("l2zld", zcl.ap, zmoe_dram[m * 128:(m + 1) * 128, sl],
                  reads=[("d", "zmoe", m, c)], writes=zcl.z)
            p.op("dve", lambda v, bk=bk: v.tensor_tensor(zcl.ap, zcl.ap, bk,
                                                         ALU.add),
                 reads=zcl.z + [zbk], writes=zcl.z)
            p.op("act", lambda a, m=m, sl=sl: a.activation(
                gout[:, m, sl], zcl.ap, AF.Silu),
                reads=zcl.z, writes=gout.z)
    dram_st(p, "l2wg", g_dram, gout, dz="l2rg")
    ar.reset(mf)

    states = ar.alloc([128, DT, T], BF16)
    u2 = ar.alloc([128, DT, T], BF16)
    g2 = ar.alloc([128, DT, T], BF16)
    inp2 = ar.alloc([128, DT, T], BF16)
    ms2 = ar.mark()
    mw2 = ar.mark()
    wi = load_w(p, ar, w["l2_WinT"], [128, DT, D])
    bic = load_f32(p, ar, w["l2_bincol"], [128, DT])

    def ev_in(a, bk, o, m):
        return a.activation(o, bk, AF.Identity, bias=bic[:, m:m + 1])
    proj(p, ps, wi, xn, inp2, DT, ev_in, bic.z, bank0=0)
    ar.reset(mw2)
    wgt = load_w(p, ar, w["l2_WgateT"], [128, DT, D])
    bgc2 = load_f32(p, ar, w["l2_bgatecol"], [128, DT])

    def ev_gate(a, bk, o, m):
        return a.activation(o, bk, AF.Sigmoid, bias=bgc2[:, m:m + 1])
    proj(p, ps, wgt, inp2, g2, DT, ev_gate, bgc2.z, bank0=2)
    ar.reset(mw2)
    bt = load_w(p, ar, w["l2_BT"], [128, DT, D])

    def ev_u(a, bk, o, m):
        return a.activation(o, bk, AF.Identity)
    proj(p, ps, bt, inp2, u2, DT, ev_u, bank0=4)
    sq_, zsq_ = ps.span(0, 4)
    for d in range(DT):
        p.op("dve", lambda v, d=d: v.tensor_tensor(
            sq_[:, 0:T], g2[:, d, :], inp2[:, d, :], ALU.mult),
            reads=g2.zt(d, DT) + inp2.zt(d, DT), writes=list(zsq_))
        p.op("dve", lambda v, d=d: v.tensor_tensor(
            inp2[:, d, :], inp2[:, d, :], sq_[:, 0:T], ALU.subtract),
            reads=inp2.zt(d, DT) + list(zsq_), writes=inp2.zt(d, DT))
    ar.reset(ms2)

    at2 = load_w(p, ar, w["l2_AT"], [128, DT, D])
    for m in range(DT):
        for c in range(TC):
            sl = slice(c * 512, (c + 1) * 512)
            bkq, zbkq = ps.bank((m * TC + c) % 2)
            mm_group(p, [(bkq, at2[:, k, m * 128:(m + 1) * 128],
                          inp2[:, k, sl], k == 0, k == DT - 1)
                         for k in range(DT)],
                     reads=at2.z + inp2.z, wzones=(zbkq,))
            p.op("dve", lambda v, bkq=bkq, m=m, sl=sl: v.tensor_tensor(
                u2[:, m, sl], u2[:, m, sl], bkq, ALU.add),
                reads=u2.zt(m, DT) + [zbkq], writes=u2.zt(m, DT))
    blend = ar.alloc([128, DT], BF16)
    ztmp = ar.alloc([128, DT], F32)
    szero = ar.alloc([128, DT], BF16)
    p.op("dve", lambda v: v.memset(szero.ap, 0.0), writes=szero.z)
    ZSC = ("l2", "scan")
    for t in range(T):
        prev = szero.ap if t == 0 else states.ap[:, :, t - 1]
        p.op("dve", lambda v, t=t, prev=prev: v.tensor_tensor(
            blend.ap, g2.ap[:, :, t], prev, ALU.mult),
            reads=g2.z + states.z + szero.z + [ZSC], writes=blend.z)
        bk, zbk = ps.bank(0, cols=DT)
        items = []
        for jt in range(DT):
            for it in range(DT):
                items.append((bk[:, jt:jt + 1],
                              at2[:, it, jt * 128:(jt + 1) * 128],
                              blend[:, it:it + 1], it == 0, it == DT - 1))
        mm_group(p, items, reads=at2.z + blend.z, wzones=(zbk,))
        p.op("dve", lambda v, t=t, bk=bk: v.tensor_tensor(
            ztmp.ap, bk, u2.ap[:, :, t], ALU.add),
            reads=[zbk] + u2.z, writes=ztmp.z)
        p.op("act", lambda a, t=t: a.activation(states.ap[:, :, t], ztmp.ap,
                                                AF.Tanh),
             reads=ztmp.z, writes=[ZSC])
    ar.reset(ms2)

    wo = load_w(p, ar, w["l2_WoutT"], [128, DT, D])
    boc = load_f32(p, ar, w["l2_boutcol"], [128, DT])
    yout = u2

    def ev_o(a, bk, o, m):
        return a.activation(o, bk, AF.Identity, bias=boc[:, m:m + 1])
    proj(p, ps, wo, Buf(states.ap, states.zones + (ZSC,)), yout, DT,
         ev_o, boc.z, bank0=0)
    dram_st(p, "l2ws", s_dram, yout, dz="l2rs")
    ar.reset(m0)

    phase_residual(p, ps, ar, "l2r", x_dram, s_dram, g_dram)


def prep_l2(inp):
    w = {}
    gw = np.zeros((1024, 128), np.float32)
    gw[:, 0:8] = np.asarray(inp["L2_ffn_gate_w"]).T
    w["l2_gatewT"] = lhsT_host(gw)
    w["l2_W1T"] = np.ascontiguousarray(
        np.stack([lhsT_host(np.ascontiguousarray(np.asarray(inp["L2_ffn_W1"][e]).T))
                  for e in range(8)]))
    w["l2_W2T"] = np.ascontiguousarray(
        np.stack([np.ascontiguousarray(np.asarray(inp["L2_ffn_W2"][e]).T)
                  .reshape(6, 128, 1024).transpose(1, 0, 2)
                  for e in range(8)]))
    w["l2_Ws1T"] = lhsT_host(np.ascontiguousarray(np.asarray(inp["L2_ffn_Ws1"]).T))
    ws2t = np.ascontiguousarray(np.asarray(inp["L2_ffn_Ws2"]).T)
    w["l2_Ws2T"] = np.ascontiguousarray(
        ws2t.reshape(12, 128, 1024).transpose(1, 0, 2))
    w["l2_WinT"] = lhsT_host(np.ascontiguousarray(np.asarray(inp["L2_attn_W_in"]).T))
    w["l2_WgateT"] = lhsT_host(np.ascontiguousarray(np.asarray(inp["L2_attn_W_gate"]).T))
    w["l2_BT"] = lhsT_host(np.ascontiguousarray(np.asarray(inp["L2_attn_B"]).T))
    w["l2_AT"] = lhsT_host(np.ascontiguousarray(np.asarray(inp["L2_attn_A"]).T))
    w["l2_WoutT"] = lhsT_host(np.ascontiguousarray(np.asarray(inp["L2_attn_W_out"]).T))
    w["l2_bincol"] = col_host(np.asarray(inp["L2_attn_b_in"]))
    w["l2_bgatecol"] = col_host(np.asarray(inp["L2_attn_b_gate"]))
    w["l2_boutcol"] = col_host(np.asarray(inp["L2_attn_b_out"]))
    w["identity"] = np.eye(128, dtype=np.float32)
    return w


L2_WSHAPES = dict(
    l2_gatewT=[128, DT, 128], l2_W1T=[8, 128, DT, 1536], l2_W2T=[8, 128, 6, 1024],
    l2_Ws1T=[128, DT, 3072], l2_Ws2T=[128, 12, 1024],
    l2_WinT=[128, DT, D], l2_WgateT=[128, DT, D], l2_BT=[128, DT, D],
    l2_AT=[128, DT, D], l2_WoutT=[128, DT, D],
    l2_bincol=[128, DT], l2_bgatecol=[128, DT], l2_boutcol=[128, DT],
    identity=[128, 128],
)


def build_layer2_test(nsteps=T):
    nc = bass.Bass("TRN2", target_bir_lowering=False, debug=False)
    x = nc.dram_tensor("x", [D, T], F32, kind="ExternalInput").ap()
    w = {k: nc.dram_tensor(k, shp, F32, kind="ExternalInput").ap()
         for k, shp in L2_WSHAPES.items()}
    xout = nc.dram_tensor("xout", [D, T], F32, kind="ExternalOutput").ap()
    s_d = nc.dram_tensor("s_scr", [D, T], BF16, kind="ExternalOutput").ap()
    g_d = nc.dram_tensor("g_scr", [D, T], BF16, kind="ExternalOutput").ap()
    zmoe = nc.dram_tensor("zmoe", [D, T], F32, kind="Internal").ap()
    wrt = nc.dram_tensor("wrt", [8, T], BF16, kind="Internal").ap()
    xi = nc.dram_tensor("x_int", [D, T], F32, kind="Internal").ap()
    p = Prog(nc)
    ps = Psum(nc)
    ar = Arena(nc, kb=202)
    tmp = ar.alloc([128, DT, T], F32)
    for d in range(DT):
        p.dma("xc", tmp[:, d, :], x[d * 128:(d + 1) * 128, :],
              writes=tmp.zt(d, DT))
        p.dma("xc2", xi[d * 128:(d + 1) * 128, :], tmp[:, d, :],
              reads=tmp.zt(d, DT), writes=[("d", "x", d)])
    ar.reset(0)
    global T_SCAN_STEPS
    layer2(p, ps, ar, w, xi, s_d, g_d, zmoe, wrt)
    tmp2 = ar.alloc([128, DT, T], F32)
    for d in range(DT):
        p.dma("xo", tmp2[:, d, :], xi[d * 128:(d + 1) * 128, :],
              reads=[("d", "x", d)], writes=tmp2.zt(d, DT))
        p.dma("xo2", xout[d * 128:(d + 1) * 128, :], tmp2[:, d, :],
              reads=tmp2.zt(d, DT))
    p.emit()
    return nc



def moe8(p, ps, ar, nm, xn, w, w1key, w2key, b1key, b2key, wrt_dram,
         wrtz, act_fn, out_cb, w1b=None, w2b=None, wz=()):
    CH = 1024
    NCH = T // CH
    m0 = ar.mark()
    hbuf = ar.alloc([128, 32, CH], BF16)
    zsb = ar.alloc([128, DT, CH], F32)
    NBUF = 3
    w1sl = [ar.alloc([128, DT, 512], BF16) for _ in range(NBUF)]
    w2sl = [ar.alloc([128, 4, D], BF16) for _ in range(NBUF)]
    b2l = ar.alloc([128, D], BF16)
    b1c = ar.alloc([128, 8, 32], F32)
    wbc = ar.alloc([128, CH], BF16)
    p.dma(nm + "b2", b2l[0:8, :], w[b2key], writes=b2l.z, queue="gp")
    p.dma("lload", b1c.ap, w[b1key], writes=b1c.z, queue="sp")

    loads = []
    for c in range(NCH):
        for e in range(8):
            for mg in range(8):
                loads.append(("w1", e, mg))
            for half in range(2):
                for kg in range(8):
                    loads.append(("w2", e, kg))
    cur = [0]
    kcnt = {"w1": 0, "w2": 0}
    parity_of = {}
    qtog = [0]

    def issue_load():
        if cur[0] >= len(loads):
            return
        kind, e_, i_ = loads[cur[0]]
        pi = kcnt[kind] % NBUF
        parity_of[cur[0]] = pi
        kcnt[kind] += 1
        cur[0] += 1
        if kind == "w1":
            src = w1b[e_, i_] if w1b is not None else w[w1key][e_, i_]
            qtog[0] ^= 1
            p.dma(nm + "w1" + "abc"[pi] + ("s" if qtog[0] else "g"),
                  w1sl[pi].ap, src,
                  reads=[("pc", nm + "w1", e_, i_)] if w1b is not None else [],
                  writes=w1sl[pi].z,
                  queue=("sp" if (w1b is not None and qtog[0]) else "gp"))
        else:
            src = w2b[e_, i_] if w2b is not None else w[w2key][e_, i_]
            qtog[0] ^= 1
            p.dma(nm + "w2" + "abc"[pi] + ("s" if qtog[0] else "g"),
                  w2sl[pi].ap, src,
                  reads=[("pc", nm + "w2", e_, i_)] if w2b is not None else [],
                  writes=w2sl[pi].z,
                  queue=("sp" if (w2b is not None and qtog[0]) else "gp"))

    issue_load()
    issue_load()
    li = 0
    for c in range(NCH):
        sl = slice(c * CH, (c + 1) * CH)
        p.dma(nm + "wr", wbc[0:8, 0:CH], wrt_dram[:, sl],
              reads=[wrtz], writes=wbc.z)
        for m in range(DT):
            for half in range(2):
                bkz, zbkz = ps.bank(6 + half)
                mm_group(p, [(bkz, b2l[0:8, m * 128:(m + 1) * 128],
                              wbc[0:8, half * 512:(half + 1) * 512],
                              True, True)],
                         reads=b2l.z + wbc.z, wzones=(zbkz,))
                p.op("dve", lambda v, m=m, half=half, bkz=bkz: v.tensor_copy(
                    zsb[:, m, half * 512:(half + 1) * 512], bkz),
                    reads=[zbkz], writes=zsb.z)
        for e in range(8):
            p.dma(nm + "wbc", wbc.ap, wrt_dram[e:e + 1, sl].to_broadcast([128, CH]),
                  reads=[wrtz], writes=wbc.z)
            for mg in range(8):
                issue_load()
                pi = parity_of[li]
                li += 1
                for mm_ in range(4):
                    m = mg * 4 + mm_
                    for half in range(2):
                        bk, zbk = ps.bank((mm_ * 2 + half) % 4)
                        mm_group(p, [(bk, w1sl[pi][:, k, mm_ * 128:(mm_ + 1) * 128],
                                      xn[:, k, c * CH + half * 512:
                                         c * CH + (half + 1) * 512],
                                      k == 0, k == DT - 1) for k in range(DT)],
                                 reads=w1sl[pi].z + xn.z, wzones=(zbk,))
                        p.op("act", lambda a, bk=bk, m=m, half=half, e=e:
                             a.activation(
                                 hbuf[:, m, half * 512:(half + 1) * 512], bk,
                                 act_fn, bias=b1c[:, e, m:m + 1]),
                             reads=[zbk] + b1c.z, writes=hbuf.zt(m, 32))
            for half in range(2):
                hs = slice(half * 512, (half + 1) * 512)
                bks = [ps.bank(i) for i in range(8)]
                for kg in range(8):
                    issue_load()
                    pi = parity_of[li]
                    li += 1
                    hz = (hbuf.zt(kg * 4, 32) + hbuf.zt(kg * 4 + 1, 32)
                          + hbuf.zt(kg * 4 + 2, 32) + hbuf.zt(kg * 4 + 3, 32))
                    for m in range(DT):
                        bk, zbk = bks[m]
                        mm_group(p, [(bk, w2sl[pi][:, j, m * 128:(m + 1) * 128],
                                      hbuf[:, kg * 4 + j, hs], kg == 0 and j == 0,
                                      kg == 7 and j == 3) for j in range(4)],
                                 reads=w2sl[pi].z + hz, wzones=(zbk,))
                for m in range(DT):
                    bk, zbk = bks[m]
                    p.op("dve", lambda v, bk=bk, m=m, hs=hs: v.tensor_tensor(
                        bk, bk, wbc[:, hs], ALU.mult),
                        reads=[zbk] + wbc.z, writes=(zbk,))
                    p.op("dve", lambda v, bk=bk, m=m, hs=hs: v.tensor_tensor(
                        zsb[:, m, hs], zsb[:, m, hs], bk, ALU.add),
                        reads=zsb.z + [zbk], writes=zsb.z)
        for m in range(DT):
            out_cb(m, c, zsb)
    ar.reset(m0)


def routing_from_logits(p, ps, ar, nm, lg, ident, wrt_dram, k_sel, renorm):
    m0 = ar.mark()
    wtok = ar.alloc([128, 16, 8], BF16)
    rcol = ar.alloc([128, 6], F32)
    rscr = ar.alloc([128, 16], BF16)
    wT = ar.alloc([128, T], BF16)
    for tt in range(16):
        sl = slice(tt * 128, (tt + 1) * 128)
        bk, zbk = ps.bank(2)
        bkv = bk.bitcast(BF16)[:, 0:8]
        p.op("pe", lambda t, sl=sl, bkv=bkv: t.transpose(
            bkv, lg[0:8, sl], ident[0:8, 0:8]),
            reads=lg.z + ident.z, writes=(zbk,))
        p.op("dve", lambda v, tt=tt, bkv=bkv: v.tensor_copy(wtok[:, tt, :], bkv),
             reads=[zbk], writes=wtok.z)
        p.op("dve", lambda v, tt=tt: v.reduce_max(rcol[:, 0:1], wtok[:, tt, :],
                                                  axis=AX.X, negate=True),
             reads=wtok.z, writes=rcol.z)
        p.op("act", lambda a, tt=tt: a.activation(
            wtok[:, tt, :], wtok[:, tt, :], AF.Exp, bias=rcol[:, 0:1],
            accum_out=rcol[:, 1:2]),
            reads=wtok.z + rcol.z, writes=wtok.z + rcol.z)
        p.op("dve", lambda v: v.reciprocal(rcol[:, 2:3], rcol[:, 1:2]),
             reads=rcol.z, writes=rcol.z)
        p.op("dve", lambda v, tt=tt: v.tensor_scalar(
            wtok[:, tt, :], wtok[:, tt, :], rcol[:, 2:3], None, ALU.mult),
            reads=wtok.z + rcol.z, writes=wtok.z)
        p.op("dve", lambda v, tt=tt: v.reduce_max(rcol[:, 3:4], wtok[:, tt, :],
                                                  axis=AX.X),
             reads=wtok.z, writes=rcol.z)
        if k_sel == 1:
            p.op("dve", lambda v, tt=tt: v.tensor_scalar(
                rscr[:, 0:8], wtok[:, tt, :], rcol[:, 3:4], None, ALU.is_ge),
                reads=wtok.z + rcol.z, writes=rscr.z)
        else:
            p.op("dve", lambda v, tt=tt: v.tensor_scalar(
                rscr[:, 0:8], wtok[:, tt, :], rcol[:, 3:4], None, ALU.is_equal),
                reads=wtok.z + rcol.z, writes=rscr.z)
            p.op("dve", lambda v, tt=tt: v.scalar_tensor_tensor(
                rscr[:, 8:16], rscr[:, 0:8], NEG, wtok[:, tt, :],
                ALU.mult, ALU.add),
                reads=rscr.z + wtok.z, writes=rscr.z)
            p.op("dve", lambda v: v.reduce_max(rcol[:, 4:5], rscr[:, 8:16],
                                               axis=AX.X),
                 reads=rscr.z, writes=rcol.z)
            p.op("dve", lambda v, tt=tt: v.tensor_scalar(
                rscr[:, 0:8], wtok[:, tt, :], rcol[:, 4:5], None, ALU.is_ge),
                reads=wtok.z + rcol.z, writes=rscr.z)
        p.op("dve", lambda v, tt=tt: v.tensor_tensor(
            wtok[:, tt, :], wtok[:, tt, :], rscr[:, 0:8], ALU.mult),
            reads=wtok.z + rscr.z, writes=wtok.z)
        if renorm:
            p.op("dve", lambda v, tt=tt: v.reduce_sum(rcol[:, 5:6],
                                                      wtok[:, tt, :], axis=AX.X),
                 reads=wtok.z, writes=rcol.z)
            p.op("dve", lambda v: v.reciprocal(rcol[:, 5:6], rcol[:, 5:6]),
                 reads=rcol.z, writes=rcol.z)
            p.op("dve", lambda v, tt=tt: v.tensor_scalar(
                wtok[:, tt, :], wtok[:, tt, :], rcol[:, 5:6], None, ALU.mult),
                reads=wtok.z + rcol.z, writes=wtok.z)
        bk3, zbk3 = ps.bank(4)
        bk3v = bk3.bitcast(BF16)[0:8, 0:128]
        p.op("pe", lambda t, tt=tt, bk3v=bk3v: t.transpose(
            bk3v, wtok[:, tt, :], ident.ap),
            reads=wtok.z + ident.z, writes=(zbk3,))
        p.op("dve", lambda v, sl=sl, bk3v=bk3v: v.tensor_copy(wT[0:8, sl], bk3v),
             reads=[zbk3], writes=wT.z)
    p.dma(nm + "wrt", wrt_dram, wT[0:8, :], reads=wT.z,
          writes=[("d", nm + "wrt")])
    ar.reset(m0)


def routing_topk(p, ps, ar, nm, wswT, bswcol, xn, ident, wrt_dram, k_sel,
                 renorm):
    m0 = ar.mark()
    lg = ar.alloc([128, T], BF16)
    for c in range(TC):
        sl = slice(c * 512, (c + 1) * 512)
        bk, zbk = ps.bank(c % 2)
        mm_group(p, [(bk, wswT[:, k, :], xn[:, k, sl], k == 0, k == DT - 1)
                     for k in range(DT)],
                 reads=wswT.z + xn.z, wzones=(zbk,))
        p.op("act", lambda a, bk=bk, sl=sl: a.activation(
            lg[:, sl], bk, AF.Identity, bias=bswcol[:, 0:1]),
            reads=[zbk] + bswcol.z, writes=lg.z)
    routing_from_logits(p, ps, ar, nm, lg, ident, wrt_dram, k_sel, renorm)
    ar.reset(m0)



def layer3(p, ps, ar, w, x_dram, s_dram, g_dram, wrt_dram, dbg=None):
    m0 = ar.mark()
    xn, _ = phase_ln(p, ps, ar, "l3n", x_dram)
    ident = load_w(p, ar, w["identity"], [128, 128])

    mf = ar.mark()
    wsw = load_w(p, ar, w["l3_switchT"], [128, DT, 128])
    bsw = load_f32(p, ar, w["l3_switchbcol"], [128, 1])
    routing_topk(p, ps, ar, "l3", wsw, bsw, xn, ident, wrt_dram, 1, False)
    ar.reset(mf)

    gst = ar.alloc([128, 1024], BF16)

    def l3_out(m, c, zsb):
        sl3 = slice(c * 1024, (c + 1) * 1024)
        p.op("act", lambda a, m=m: a.activation(
            gst.ap, zsb[:, m, :], AF.Silu),
            reads=zsb.z, writes=gst.z)
        p.dma("l3wg", g_dram[m * 128:(m + 1) * 128, sl3], gst.ap,
              reads=gst.z, writes=[("d", "l3rg", m)])

    moe8(p, ps, ar, "l3", xn, w, "l3_w1", "l3_w2", "l3_b1cols", "L3_ffn_b2",
         wrt_dram, ("d", "l3wrt"), AF.Gelu_apprx_tanh, l3_out,
         w1b=w.get("__m_l3w1"), w2b=w.get("__m_l3w2"))
    ar.reset(mf)

    att = ar.alloc([128, DT, T], BF16)
    ma = ar.mark()
    qkvw = load_w(p, ar, w["l3_qkvT"], [128, DT, 1536])
    qkvb = load_f32(p, ar, w["l3_qkvbcol"], [128, 12])
    qT = ar.alloc([128, DT, T], BF16)
    kvT = ar.alloc([128, 4, T], BF16)
    scl2 = load_f32(p, ar, w["l3_scaling2"], [128, 1])

    for m in range(12):
        for c in range(TC):
            bk, zbk = ps.bank((m * TC + c) % 2)
            sl = slice(c * 512, (c + 1) * 512)
            mm_group(p, [(bk, qkvw[:, k, m * 128:(m + 1) * 128],
                          xn[:, k, sl], k == 0, k == DT - 1)
                         for k in range(DT)],
                     reads=qkvw.z + xn.z, wzones=(zbk,))
            if m < 8:
                p.op("act", lambda a, bk=bk, m=m, sl=sl: a.activation(
                    qT[:, m, sl], bk, AF.Identity, bias=qkvb[:, m:m + 1]),
                    reads=[zbk] + qkvb.z, writes=qT.z)
            else:
                p.op("act", lambda a, bk=bk, m=m, sl=sl: a.activation(
                    kvT[:, m - 8, sl], bk, AF.Identity, bias=qkvb[:, m:m + 1]),
                    reads=[zbk] + qkvb.z, writes=kvT.z)
    spc = ar.alloc([128, 4], F32)
    p.op("act", lambda a: a.activation(spc[:, 0:1], scl2.ap, AF.Abs),
         reads=scl2.z, writes=spc.z)
    p.op("act", lambda a: a.activation(spc[:, 1:2], spc[:, 0:1], AF.Exp,
                                       scale=-1.0),
         reads=spc.z, writes=spc.z)
    onec, zone1 = p.ccol(1.0)
    p.op("act", lambda a: a.activation(spc[:, 1:2], spc[:, 1:2], AF.Ln,
                                       bias=onec),
         reads=spc.z + [zone1], writes=spc.z)
    p.op("dve", lambda v: v.tensor_scalar(spc[:, 2:3], scl2.ap, 0.0, None,
                                          ALU.max),
         reads=scl2.z, writes=spc.z)
    p.op("dve", lambda v: v.tensor_tensor(spc[:, 3:4], spc[:, 1:2],
                                          spc[:, 2:3], ALU.add),
         reads=spc.z, writes=spc.z)
    p.op("dve", lambda v: v.tensor_scalar(spc[:, 3:4], spc[:, 3:4],
                                          1.442695041 / 8.0, None, ALU.mult),
         reads=spc.z, writes=spc.z)
    for m in range(DT):
        p.op("dve", lambda v, m=m: v.tensor_scalar(
            qT[:, m, :], qT[:, m, :], spc[:, 3:4], None, ALU.mult),
            reads=qT.z + spc.z, writes=qT.z)
    vtm = ar.alloc([128, 16, 256], BF16)
    for tt in range(16):
        for vt in range(2):
            bk, zbk = ps.bank(4)
            bkv = bk.bitcast(BF16)[:, 0:128]
            p.op("pe", lambda t, tt=tt, vt=vt, bkv=bkv: t.transpose(
                bkv, kvT[:, 2 + vt, tt * 128:(tt + 1) * 128], ident.ap),
                reads=kvT.z + ident.z, writes=(zbk,))
            p.op("dve", lambda v, tt=tt, vt=vt, bkv=bkv: v.tensor_copy(
                vtm[:, tt, vt * 128:(vt + 1) * 128], bkv),
                reads=[zbk], writes=vtm.z)
    if dbg is not None:
        p.dma("dbvt", dbg["vtm"], vtm.ap.rearrange("p a b -> p (a b)"),
              reads=vtm.z)
    probs = [ar.alloc([128, T], BF16) for _ in range(2)]
    pT = [ar.alloc([128, 16, 128], BF16) for _ in range(2)]
    rst = [ar.alloc([128, 2], F32) for _ in range(2)]
    qs = [ar.alloc([128, T], BF16) for _ in range(2)]
    ks = [ar.alloc([128, T], BF16) for _ in range(2)]
    it = 0
    pend_t = [None]
    pend_c = [None]
    pend_a = [None, None]

    def fl(slot):
        if slot[0] is not None:
            slot[0]()
            slot[0] = None

    for h in range(16):
        kh = h // 4
        qrow = (h % 2) * 64
        qt_ = h // 2
        hp = h % 2
        p.op("dve", lambda v, qrow=qrow, qt_=qt_, hp=hp: v.tensor_copy(
            qs[hp][0:64, :], qT.ap[qrow:qrow + 64, qt_, :]),
            reads=qT.z, writes=qs[hp].z)
        p.op("dve", lambda v, kh=kh, hp=hp: v.tensor_copy(
            ks[hp][0:64, :], kvT.ap[(kh % 2) * 64:(kh % 2) * 64 + 64, kh // 2, :]),
            reads=kvT.z, writes=ks[hp].z)
        for tqt in range(16):
            pi = it % 2
            it += 1
            pr, pt, rsl = probs[pi], pT[pi], rst[pi]
            ssp, zssp = ps.span(0 if pi == 0 else 4, 4)
            tq = slice(tqt * 128, (tqt + 1) * 128)
            fl(pend_a)
            pend_a[0] = pend_a[1]
            pend_a[1] = None
            for c in range(TC):
                sl = slice(c * 512, (c + 1) * 512)
                mm_group(p, [(ssp[:, sl], qs[hp].ap[0:64, tq],
                              ks[hp].ap[0:64, sl], True, True)],
                         reads=qs[hp].z + ks[hp].z, wzones=(zssp[c],))
            fl(pend_t)
            p.op("dve", lambda v, ssp=ssp, rsl=rsl: v.reduce_max(
                rsl[:, 0:1], ssp[:, 0:T], axis=AX.X, negate=True),
                reads=list(zssp), writes=rsl.z)
            p.op("act", lambda a, ssp=ssp, pr=pr, rsl=rsl: a.activation(
                pr.ap, ssp[:, 0:T], AF.Exp, bias=rsl[:, 0:1],
                accum_out=rsl[:, 1:2]),
                reads=list(zssp) + rsl.z, writes=pr.z + rsl.z)
            p.op("dve", lambda v, rsl=rsl: v.reciprocal(rsl[:, 1:2], rsl[:, 1:2]),
                 reads=rsl.z, writes=rsl.z)
            p.op("dve", lambda v, pr=pr, rsl=rsl: v.tensor_scalar(
                pr.ap, pr.ap, rsl[:, 1:2], None, ALU.mult),
                reads=pr.z + rsl.z, writes=pr.z)

            fl(pend_c)

            def stage_t(pi=pi, pr=pr):
                for i in range(16):
                    bk, zbk = ps.bank((0 if pi == 0 else 4) + i // 8)
                    bkv = bk.bitcast(BF16)[:, (i % 8) * 128:(i % 8 + 1) * 128]
                    p.op("pe", lambda t, i=i, bkv=bkv, pr=pr: t.transpose(
                        bkv, pr[:, i * 128:(i + 1) * 128], ident.ap),
                        reads=pr.z + ident.z, writes=(zbk,))

            def stage_c(pi=pi, pt=pt):
                for j in range(2):
                    bk, zbk = ps.bank((0 if pi == 0 else 4) + j)
                    bkv = bk.bitcast(BF16)[:, 0:1024]
                    p.op("dve", lambda v, j=j, bkv=bkv, pt=pt: v.tensor_copy(
                        pt.ap[:, j * 8:(j + 1) * 8, :].rearrange("p a b -> p (a b)"),
                        bkv),
                        reads=[zbk], writes=pt.z)

            def stage_a(pi=pi, pt=pt, kh=kh, qrow=qrow, qt_=qt_, tq=tq):
                bk6, zbk6 = ps.bank((2 if pi == 0 else 6), rows=64, cols=128)
                mm_group(p, [(bk6, vtm[:, i, kh * 64:(kh + 1) * 64], pt[:, i, :],
                              i == 0, i == 15) for i in range(16)],
                         reads=vtm.z + pt.z, wzones=(zbk6,))
                p.op("act", lambda a, qrow=qrow, qt_=qt_, tq=tq, bk6=bk6:
                     a.activation(att.ap[qrow:qrow + 64, qt_, tq], bk6,
                                  AF.Identity),
                     reads=[zbk6], writes=att.z)
            pend_t[0] = stage_t
            pend_c[0] = stage_c
            pend_a[1] = stage_a
    fl(pend_a)
    fl(pend_t)
    fl(pend_c)
    pend_a[0] = pend_a[1]
    fl(pend_a)
    ar.reset(ma)
    ow = load_w(p, ar, w["l3_owT"], [128, DT, D])
    obc = load_f32(p, ar, w["l3_obcol"], [128, DT])
    yout3 = ar.alloc([128, DT, T], BF16)

    def ev_o3(a, bk, o, m):
        return a.activation(o, bk, AF.Identity, bias=obc[:, m:m + 1])
    proj(p, ps, ow, att, yout3, DT, ev_o3, obc.z, bank0=0)
    dram_st(p, "l3ws", s_dram, yout3, dz="l3rs")
    ar.reset(m0)
    phase_residual(p, ps, ar, "l3r", x_dram, s_dram, g_dram)


def prep_l3(inp):
    w = {}
    w["l3_qkvT"] = lhsT_host(np.ascontiguousarray(np.asarray(inp["L3_attn_qkv_w"]).T))
    w["l3_qkvbcol"] = col_host(np.asarray(inp["L3_attn_qkv_b"]))
    w["l3_owT"] = lhsT_host(np.ascontiguousarray(np.asarray(inp["L3_attn_o_w"]).T))
    w["l3_obcol"] = col_host(np.asarray(inp["L3_attn_o_b"]))
    sc = np.asarray(inp["L3_attn_scaling"]).reshape(64)
    w["l3_scaling2"] = np.ascontiguousarray(np.concatenate([sc, sc]).reshape(128, 1))
    sw = np.zeros((1024, 128), np.float32)
    sw[:, 0:8] = np.asarray(inp["L3_ffn_switch_w"])
    w["l3_switchT"] = lhsT_host(sw)
    sb = np.zeros((128, 1), np.float32)
    sb[0:8, 0] = np.asarray(inp["L3_ffn_switch_b"])
    w["l3_switchbcol"] = sb
    w1 = np.asarray(inp["L3_ffn_w1"])
    w["l3_w1"] = np.ascontiguousarray(
        np.stack([w1[e].reshape(8, 128, 8, 512).transpose(2, 1, 0, 3)
                  for e in range(8)]))
    w2 = np.asarray(inp["L3_ffn_w2"])
    w["l3_w2"] = np.ascontiguousarray(
        np.stack([w2[e].reshape(8, 4, 128, 1024).transpose(0, 2, 1, 3)
                  for e in range(8)]))
    w["l3_b1cols"] = np.ascontiguousarray(
        np.stack([col_host(np.asarray(inp["L3_ffn_b1"][e])) for e in range(8)]))
    w["L3_ffn_b2"] = np.asarray(inp["L3_ffn_b2"])
    w["identity"] = np.eye(128, dtype=np.float32)
    return w


L3_WSHAPES = dict(
    l3_qkvT=[128, DT, 1536], l3_qkvbcol=[128, 12], l3_owT=[128, DT, D],
    l3_obcol=[128, DT], l3_scaling2=[128, 1], l3_switchT=[128, DT, 128],
    l3_switchbcol=[128, 1], l3_w1=[8, 8, 128, 8, 512],
    l3_w2=[8, 8, 128, 4, 1024], l3_b1cols=[8, 128, 32], L3_ffn_b2=[8, 1024],
    identity=[128, 128],
)


def build_layer3_test(debug=False):
    nc = bass.Bass("TRN2", target_bir_lowering=False, debug=False)
    x = nc.dram_tensor("x", [D, T], F32, kind="ExternalInput").ap()
    w = {k: nc.dram_tensor(k, shp, F32, kind="ExternalInput").ap()
         for k, shp in L3_WSHAPES.items()}
    xout = nc.dram_tensor("xout", [D, T], F32, kind="ExternalOutput").ap()
    s_d = nc.dram_tensor("s_scr", [D, T], BF16, kind="ExternalOutput").ap()
    g_d = nc.dram_tensor("g_scr", [D, T], BF16, kind="ExternalOutput").ap()
    wrt = nc.dram_tensor("wrt", [8, T], BF16, kind="Internal").ap()
    xi = nc.dram_tensor("x_int", [D, T], F32, kind="Internal").ap()
    dbg = None
    if debug:
        dbg = dict(qT=nc.dram_tensor("dbg_qT", [D, T], BF16, kind="ExternalOutput").ap(),
                   kvT=nc.dram_tensor("dbg_kvT", [512, T], BF16, kind="ExternalOutput").ap(),
                   att=nc.dram_tensor("dbg_att", [D, T], BF16, kind="ExternalOutput").ap(),
                   sc0=nc.dram_tensor("dbg_sc0", [128, T], F32, kind="ExternalOutput").ap(),
                   pr0=nc.dram_tensor("dbg_pr0", [128, T], BF16, kind="ExternalOutput").ap(),
                   pt0=nc.dram_tensor("dbg_pt0", [128, 2048], BF16, kind="ExternalOutput").ap(),
                   vtm=nc.dram_tensor("dbg_vtm", [128, 4096], BF16, kind="ExternalOutput").ap(),
                   av0=nc.dram_tensor("dbg_av0", [64, 128], F32, kind="ExternalOutput").ap(),
                   av1=nc.dram_tensor("dbg_av1", [64, 128], F32, kind="ExternalOutput").ap())
    p = Prog(nc)
    ps = Psum(nc)
    ar = Arena(nc, kb=202)
    tmp = ar.alloc([128, DT, T], F32)
    for d in range(DT):
        p.dma("xc", tmp[:, d, :], x[d * 128:(d + 1) * 128, :],
              writes=tmp.zt(d, DT))
        p.dma("xc2", xi[d * 128:(d + 1) * 128, :], tmp[:, d, :],
              reads=tmp.zt(d, DT), writes=[("d", "x", d)])
    ar.reset(0)
    layer3(p, ps, ar, w, xi, s_d, g_d, wrt, dbg=dbg)
    tmp2 = ar.alloc([128, DT, T], F32)
    for d in range(DT):
        p.dma("xo", tmp2[:, d, :], xi[d * 128:(d + 1) * 128, :],
              reads=[("d", "x", d)], writes=tmp2.zt(d, DT))
        p.dma("xo2", xout[d * 128:(d + 1) * 128, :], tmp2[:, d, :],
              reads=tmp2.zt(d, DT))
    p.emit()
    return nc



SPAD = 1280


def layer0(p, ps, ar, w, x_dram, s_dram, g_dram, scr, noise_dram):
    m0 = ar.mark()
    xn, xnf = phase_ln(p, ps, ar, "l0n", x_dram, want_f32=True)
    if getattr(p, "_pending_precast", None) is not None:
        pc_ = p._pending_precast
        p._pending_precast = None
        mpc = ar.mark()
        pc_()
        ar.reset(mpc)
    dram_st(p, "l0xnf", scr["xnf"], xnf, dz="l0xnf")
    ar.reset(xnf.off)

    ma = ar.mark()
    qb = ar.alloc([128, DT, T], BF16)
    kb = ar.alloc([128, DT, T], BF16)
    mw = ar.mark()
    wq = load_w(p, ar, w["l0_WqT"], [128, DT, D])
    bq8 = load_f32(p, ar, w["l0_bqcol"], [128, DT])
    p.op("dve", lambda v: v.tensor_scalar(bq8.ap, bq8.ap, 0.125, None, ALU.mult),
         reads=bq8.z, writes=bq8.z)

    def ev_q(a, bk, o, m):
        return a.activation(o, bk, AF.Identity, bias=bq8[:, m:m + 1], scale=0.125)
    proj(p, ps, wq, xn, qb, DT, ev_q, bq8.z, bank0=0)
    ar.reset(mw)
    wk = load_w(p, ar, w["l0_WkT"], [128, DT, D])

    def ev_cp0(a, bk, o, m):
        return a.activation(o, bk, AF.Identity)
    proj(p, ps, wk, xn, kb, DT, ev_cp0, bank0=2)
    ar.reset(mw)
    dram_st(p, "l0q", scr["q0"], qb, dz="l0q")
    dram_st(p, "l0k", scr["k0"], kb, dz="l0k")
    vtm = ar.alloc([128, 16, D], BF16)
    wv = load_w(p, ar, w["L0_attn_Wv"], [128, DT, D])
    for tt in range(16):
        for ch in range(2):
            bk, zbk = ps.bank(4 + (tt * 2 + ch) % 2)
            mm_group(p, [(bk, xn[:, k, tt * 128:(tt + 1) * 128],
                          wv[:, k, ch * 512:(ch + 1) * 512],
                          k == 0, k == DT - 1) for k in range(DT)],
                     reads=xn.z + wv.z, wzones=(zbk,))
            p.op("act", lambda a, bk=bk, tt=tt, ch=ch: a.activation(
                vtm[:, tt, ch * 512:(ch + 1) * 512], bk, AF.Identity),
                reads=[zbk], writes=vtm.z)
    p.dma("l0v", scr["v0"], vtm.ap.rearrange("p a b -> p (a b)"),
          reads=vtm.z, writes=[("d", "l0v")])
    ar.reset(mw)
    rb = ar.alloc([128, DT, T], BF16)
    wr = load_w(p, ar, w["l0_WrT"], [128, DT, D])
    brc = load_f32(p, ar, w["l0_brcol"], [128, DT])

    def ev_r(a, bk, o, m):
        return a.activation(o, bk, AF.Identity, bias=brc[:, m:m + 1])
    proj(p, ps, wr, xn, rb, DT, ev_r, brc.z, bank0=0)
    bnd = ar.alloc([128, D], BF16)
    p.dma("wload", bnd[0:10, :], w["l0_bndrev"], writes=bnd.z, queue="gp")
    rs = ar.alloc([128, T], BF16)
    negc = ar.alloc([128, 16, 128], BF16)
    p.op("dve", lambda v: v.memset(negc.ap, NEG), writes=negc.z)
    es = ar.alloc([128, D], BF16)
    for h in range(16):
        base = (h % 2) * 64
        p.op("dve", lambda v, h=h, base=base, rs=rs: v.tensor_copy(
            rs[0:10, :], rb.ap[base:base + 10, h // 2, :]),
            reads=rb.z, writes=rs.z)
        p.dma("l0ep", scr["epad"][h, :, 0:128].rearrange(
            "(a b) c -> b a c", b=128), negc.ap, reads=negc.z,
            writes=[("d", "ep", h)])
        p.dma("l0ep", scr["epad"][h, :, 1152:1280].rearrange(
            "(a b) c -> b a c", b=128), negc.ap, reads=negc.z,
            writes=[("d", "ep", h)])
        for tt in range(16):
            spn, zspn = ps.span(0, 2)
            mm_group(p, [(spn[:, c * 512:(c + 1) * 512], rs[0:10, tt * 128:(tt + 1) * 128],
                          bnd[0:10, c * 512:(c + 1) * 512], True, True)
                         for c in range(2)],
                     reads=rs.z + bnd.z, wzones=zspn)
            p.op("act", lambda a, spn=spn: a.activation(es.ap, spn[:, 0:1024],
                                                        AF.Identity),
                 reads=list(zspn), writes=es.z)
            p.dma("l0ep", scr["epad"][h, tt * 128:(tt + 1) * 128, 128:1152],
                  es.ap, reads=es.z, writes=[("d", "ep", h)])
    ar.reset(ma)

    ident = load_w(p, ar, w["identity"], [128, 128])
    att2 = ar.alloc([128, DT, T], BF16)
    k2 = ar.alloc([128, DT, T], BF16)
    v2tm = ar.alloc([128, 16, D], BF16)
    q2 = ar.alloc([128, DT, T], BF16)
    mb = ar.mark()
    ckv = ar.alloc([128, 4, T], BF16)
    mbw = ar.mark()
    wdkv = load_w(p, ar, w["L0_ffn_Wdkv"], [128, DT, 512])
    bdkvc = load_f32(p, ar, w["l0_bdkvcol"], [128, 4])
    for m in range(4):
        for c in range(TC):
            sl = slice(c * 512, (c + 1) * 512)
            bk, zbk = ps.bank((m * TC + c) % 2)
            mm_group(p, [(bk, wdkv[:, k, m * 128:(m + 1) * 128], xn[:, k, sl],
                          k == 0, k == DT - 1) for k in range(DT)],
                     reads=wdkv.z + xn.z, wzones=(zbk,))
            p.op("act", lambda a, bk=bk, m=m, sl=sl: a.activation(
                ckv[:, m, sl], bk, AF.Identity, bias=bdkvc[:, m:m + 1]),
                reads=[zbk] + bdkvc.z, writes=ckv.z)
    ar.reset(mbw)
    wuk = load_w(p, ar, w["L0_ffn_Wuk"], [128, 4, D])
    bukc = load_f32(p, ar, w["l0_bukcol"], [128, DT])

    def ev_k2(a, bk, o, m):
        return a.activation(o, bk, AF.Identity, bias=bukc[:, m:m + 1])
    proj(p, ps, wuk, ckv, k2, DT, ev_k2, bukc.z, bank0=0, kt=4)
    ar.reset(mbw)
    wuv = load_w(p, ar, w["L0_ffn_Wuv"], [128, 4, D])
    buvb = ar.alloc([128, D], BF16)
    p.dma("wload", buvb[0:1, :], w["l0_buvrow"], writes=buvb.z, queue="gp")
    onesr = ar.alloc([128, 128], BF16)
    p.op("dve", lambda v: v.memset(onesr.ap, 1.0), writes=onesr.z)
    for tt in range(16):
        for ch in range(2):
            bk, zbk = ps.bank(4 + (tt * 2 + ch) % 2)
            items = [(bk, onesr[0:1, 0:128],
                      buvb[0:1, ch * 512:(ch + 1) * 512], True, False)]
            items += [(bk, ckv[:, k, tt * 128:(tt + 1) * 128],
                       wuv[:, k, ch * 512:(ch + 1) * 512],
                       False, k == 3) for k in range(4)]
            mm_group(p, items, reads=onesr.z + buvb.z + ckv.z + wuv.z,
                     wzones=(zbk,))
            p.op("act", lambda a, bk=bk, tt=tt, ch=ch: a.activation(
                v2tm[:, tt, ch * 512:(ch + 1) * 512], bk, AF.Identity),
                reads=[zbk], writes=v2tm.z)
    ar.reset(mbw)
    cq = ar.alloc([128, 4, T], BF16)
    mbw2 = ar.mark()
    wdq = load_w(p, ar, w["L0_ffn_Wdq"], [128, DT, 512])
    bdqc = load_f32(p, ar, w["l0_bdqcol"], [128, 4])
    for m in range(4):
        for c in range(TC):
            sl = slice(c * 512, (c + 1) * 512)
            bk, zbk = ps.bank((m * TC + c) % 2)
            mm_group(p, [(bk, wdq[:, k, m * 128:(m + 1) * 128], xn[:, k, sl],
                          k == 0, k == DT - 1) for k in range(DT)],
                     reads=wdq.z + xn.z, wzones=(zbk,))
            p.op("act", lambda a, bk=bk, m=m, sl=sl: a.activation(
                cq[:, m, sl], bk, AF.Identity, bias=bdqc[:, m:m + 1]),
                reads=[zbk] + bdqc.z, writes=cq.z)
    ar.reset(mbw2)
    wuq = load_w(p, ar, w["L0_ffn_Wuq"], [128, 4, D])
    buq8 = load_f32(p, ar, w["l0_buqcol"], [128, DT])
    p.op("dve", lambda v: v.tensor_scalar(buq8.ap, buq8.ap, 0.125, None,
                                          ALU.mult),
         reads=buq8.z, writes=buq8.z)

    def ev_q2(a, bk, o, m):
        return a.activation(o, bk, AF.Identity, bias=buq8[:, m:m + 1],
                            scale=0.125)
    proj(p, ps, wuq, cq, q2, DT, ev_q2, buq8.z, bank0=2, kt=4)
    ar.reset(mb)
    probs2 = [ar.alloc([128, T], BF16) for _ in range(2)]
    pT2 = [ar.alloc([128, 16, 128], BF16) for _ in range(2)]
    rst2 = [ar.alloc([128, 2], F32) for _ in range(2)]
    qs2 = [ar.alloc([128, T], BF16) for _ in range(2)]
    ks2 = [ar.alloc([128, T], BF16) for _ in range(2)]
    cmask = load_w(p, ar, w["l0_causal"], [128, 128])
    it = 0
    pend2_t = [None]
    pend2_c = [None]
    pend2_a = [None, None]

    def fl2(slot):
        if slot[0] is not None:
            slot[0]()
            slot[0] = None

    for h in range(16):
        base = (h % 2) * 64
        hp = h % 2
        p.op("dve", lambda v, base=base, h=h, hp=hp: v.tensor_copy(
            qs2[hp][0:64, :], q2.ap[base:base + 64, h // 2, :]),
            reads=q2.z, writes=qs2[hp].z)
        p.op("dve", lambda v, base=base, h=h, hp=hp: v.tensor_copy(
            ks2[hp][0:64, :], k2.ap[base:base + 64, h // 2, :]),
            reads=k2.z, writes=ks2[hp].z)
        for tqt in range(16):
            pi = it % 2
            it += 1
            pr, pt, rsl = probs2[pi], pT2[pi], rst2[pi]
            bb = 0 if pi == 0 else 4
            t0 = tqt * 128
            W = t0 + 128
            ssp, zssp = ps.span(bb, 4)
            tq = slice(t0, t0 + 128)
            fl2(pend2_a)
            pend2_a[0] = pend2_a[1]
            pend2_a[1] = None
            for c in range((W + 511) // 512):
                c0 = c * 512
                c1 = min(c0 + 512, W)
                mm_group(p, [(ssp[:, c0:c1], qs2[hp].ap[0:64, tq],
                              ks2[hp].ap[0:64, c0:c1], True, True)],
                         reads=qs2[hp].z + ks2[hp].z, wzones=(zssp[c],))
            fl2(pend2_t)
            p.op("dve", lambda v, t0=t0, ssp=ssp: v.tensor_tensor(
                ssp[:, t0:t0 + 128], ssp[:, t0:t0 + 128], cmask.ap, ALU.add),
                reads=list(zssp) + cmask.z, writes=(zssp[t0 // 512],))
            p.op("dve", lambda v, W=W, ssp=ssp, rsl=rsl: v.reduce_max(
                rsl[:, 0:1], ssp[:, 0:W], axis=AX.X, negate=True),
                reads=list(zssp), writes=rsl.z)
            p.op("act", lambda a, W=W, ssp=ssp, pr=pr, rsl=rsl: a.activation(
                pr.ap[:, 0:W], ssp[:, 0:W], AF.Exp, bias=rsl[:, 0:1],
                accum_out=rsl[:, 1:2]),
                reads=list(zssp) + rsl.z, writes=pr.z + rsl.z)
            p.op("dve", lambda v, rsl=rsl: v.reciprocal(rsl[:, 1:2], rsl[:, 1:2]),
                 reads=rsl.z, writes=rsl.z)
            p.op("dve", lambda v, W=W, pr=pr, rsl=rsl: v.tensor_scalar(
                pr.ap[:, 0:W], pr.ap[:, 0:W], rsl[:, 1:2], None, ALU.mult),
                reads=pr.z + rsl.z, writes=pr.z)

            fl2(pend2_c)

            def stage2_t(pi=pi, pr=pr, W=W, bb=bb):
                npt = W // 128
                for i in range(npt):
                    bk, zbk = ps.bank(bb + i // 8)
                    bkv = bk.bitcast(BF16)[:, (i % 8) * 128:(i % 8 + 1) * 128]
                    p.op("pe", lambda t, i=i, bkv=bkv, pr=pr: t.transpose(
                        bkv, pr[:, i * 128:(i + 1) * 128], ident.ap),
                        reads=pr.z + ident.z, writes=(zbk,))

            def stage2_c(pi=pi, pt=pt, W=W, bb=bb):
                npt = W // 128
                for j in range((npt + 7) // 8):
                    nj = min(8, npt - j * 8)
                    bk, zbk = ps.bank(bb + j)
                    bkv = bk.bitcast(BF16)[:, 0:nj * 128]
                    p.op("dve", lambda v, j=j, nj=nj, bkv=bkv, pt=pt: v.tensor_copy(
                        pt.ap[:, j * 8:j * 8 + nj, :].rearrange("p a b -> p (a b)"),
                        bkv),
                        reads=[zbk], writes=pt.z)

            def stage2_a(pi=pi, pt=pt, h=h, base=base, tq=tq, W=W, bb=bb):
                npt = W // 128
                bk6, zbk6 = ps.bank(bb + 2, rows=64, cols=128)
                mm_group(p, [(bk6, v2tm[:, i, h * 64:(h + 1) * 64], pt[:, i, :],
                              i == 0, i == npt - 1) for i in range(npt)],
                         reads=v2tm.z + pt.z, wzones=(zbk6,))
                p.op("act", lambda a, base=base, h=h, tq=tq, bk6=bk6: a.activation(
                    att2.ap[base:base + 64, h // 2, tq], bk6, AF.Identity),
                    reads=[zbk6], writes=att2.z)
            pend2_t[0] = stage2_t
            pend2_c[0] = stage2_c
            pend2_a[1] = stage2_a
    fl2(pend2_a)
    fl2(pend2_t)
    fl2(pend2_c)
    pend2_a[0] = pend2_a[1]
    fl2(pend2_a)
    ar.reset(mb)
    wo0 = load_w(p, ar, w["L0_ffn_Wo"], [128, DT, D])
    boc0 = load_f32(p, ar, w["l0_bocol"], [128, DT])
    att2o = k2

    def ev_wo(a, bk, o, m):
        return a.activation(o, bk, AF.Identity, bias=boc0[:, m:m + 1])
    proj(p, ps, wo0, att2, att2o, DT, ev_wo, boc0.z, bank0=0)
    dram_st(p, "l0at2", scr["att2"], att2o, dz="l0at2")
    ar.reset(m0)

    x1 = ar.alloc([128, DT, T], BF16)
    mc = ar.mark()
    xsb = ar.alloc([128, DT, T], F32)
    a2b = ar.alloc([128, DT, T], BF16)
    dram_ld(p, "l0xnfl", xsb, scr["xnf"], dz="l0xnf")
    dram_ld(p, "l0at2l", a2b, scr["att2"], dz="l0at2")
    for d in range(DT):
        p.op("dve", lambda v, d=d: v.tensor_tensor(
            xsb[:, d, :], xsb[:, d, :], a2b[:, d, :], ALU.add),
            reads=xsb.zt(d, DT) + a2b.zt(d, DT), writes=xsb.zt(d, DT))
    g1c = load_f32(p, ar, w["l0_gamma1col"], [128, DT])
    b1cc = load_f32(p, ar, w["l0_beta1col"], [128, DT])
    phase_ln_core(p, ps, ar, "l0x1", xsb, out_bf=x1, gamma=g1c, beta=b1cc)
    ar.reset(mc)

    md = ar.mark()
    lg0 = ar.alloc([128, T], BF16)
    hxg = ar.alloc([128, T], F32)
    hxn = ar.alloc([128, T], F32)
    noi = ar.alloc([128, T], F32)
    wg0 = load_w(p, ar, w["l0_WgpadT"], [128, DT, 128])
    wn0 = load_w(p, ar, w["l0_WnpadT"], [128, DT, 128])
    bg0 = load_f32(p, ar, w["l0_bgpadcol"], [128, 1])
    bn0 = load_f32(p, ar, w["l0_bnpadcol"], [128, 1])
    ident2 = load_w(p, ar, w["identity"], [128, 128])
    p.dma("lload", noi[0:8, :], noise_dram, writes=noi.z, queue="sp")
    for c in range(TC):
        sl = slice(c * 512, (c + 1) * 512)
        bk, zbk = ps.bank(c % 2)
        mm_group(p, [(bk, wg0[:, k, :], x1[:, k, sl], k == 0, k == DT - 1)
                     for k in range(DT)],
                 reads=wg0.z + x1.z, wzones=(zbk,))
        p.op("act", lambda a, bk=bk, sl=sl: a.activation(
            hxg[:, sl], bk, AF.Identity, bias=bg0[:, 0:1]),
            reads=[zbk] + bg0.z, writes=hxg.z)
        bk2, zbk2 = ps.bank(2 + c % 2)
        mm_group(p, [(bk2, wn0[:, k, :], x1[:, k, sl], k == 0, k == DT - 1)
                     for k in range(DT)],
                 reads=wn0.z + x1.z, wzones=(zbk2,))
        p.op("act", lambda a, bk2=bk2, sl=sl: a.activation(
            hxn[:, sl], bk2, AF.Identity, bias=bn0[:, 0:1]),
            reads=[zbk2] + bn0.z, writes=hxn.z)
    spa = ar.alloc([128, T], F32)
    onec0, zo0 = p.ccol(1.0)
    p.op("act", lambda a: a.activation(spa[0:8, :], hxn[0:8, :], AF.Abs),
         reads=hxn.z, writes=spa.z)
    p.op("act", lambda a: a.activation(spa[0:8, :], spa[0:8, :], AF.Exp,
                                       scale=-1.0),
         reads=spa.z, writes=spa.z)
    p.op("act", lambda a: a.activation(spa[0:8, :], spa[0:8, :], AF.Ln,
                                       bias=onec0[0:8, :]),
         reads=spa.z + [zo0], writes=spa.z)
    p.op("dve", lambda v: v.tensor_scalar(hxn[0:8, :], hxn[0:8, :], 0.0, None,
                                          ALU.max),
         reads=hxn.z, writes=hxn.z)
    p.op("dve", lambda v: v.tensor_tensor(spa[0:8, :], spa[0:8, :], hxn[0:8, :],
                                          ALU.add),
         reads=spa.z + hxn.z, writes=spa.z)
    p.op("dve", lambda v: v.tensor_tensor(noi[0:8, :], noi[0:8, :], spa[0:8, :],
                                          ALU.mult),
         reads=noi.z + spa.z, writes=noi.z)
    p.op("dve", lambda v: v.tensor_tensor(lg0[0:8, :], hxg[0:8, :], noi[0:8, :],
                                          ALU.add),
         reads=hxg.z + noi.z, writes=lg0.z)
    routing_from_logits(p, ps, ar, "l0", lg0, ident2, scr["wrt0"], 2, True)
    ar.reset(md)

    y2st = ar.alloc([128, 1024], F32)

    def l0_out(m, c, zsb):
        sl0 = slice(c * 1024, (c + 1) * 1024)
        p.op("dve", lambda v, m=m, sl0=sl0: v.tensor_tensor(
            y2st.ap, zsb[:, m, :], x1.ap[:, m, sl0], ALU.add),
            reads=zsb.z + x1.z, writes=y2st.z)
        p.dma("l0y2", scr["y2"][m * 128:(m + 1) * 128, sl0], y2st.ap,
              reads=y2st.z, writes=[("d", "y2", m, c)])

    moe8(p, ps, ar, "l0", x1, w, "l0_w1", "l0_w2", "l0_b1cols", "L0_ffn_b2",
         scr["wrt0"], ("d", "l0wrt"), AF.Relu, l0_out,
         w1b=w.get("__m_l0w1"), w2b=w.get("__m_l0w2"))
    ar.reset(m0)

    gout0 = ar.alloc([128, DT, T], BF16)
    mf2 = ar.mark()
    xsb2 = ar.alloc([128, DT, T], F32)
    for d in range(DT):
        for c2 in range(2):
            p.dma("l0y2l", xsb2[:, d, c2 * 1024:(c2 + 1) * 1024],
                  scr["y2"][d * 128:(d + 1) * 128, c2 * 1024:(c2 + 1) * 1024],
                  reads=[("d", "y2", d, c2)], writes=xsb2.zt(d, DT))
    g2c = load_f32(p, ar, w["l0_gamma2col"], [128, DT])
    b2cc = load_f32(p, ar, w["l0_beta2col"], [128, DT])
    phase_ln_core(p, ps, ar, "l0g", xsb2, out_bf=gout0, gamma=g2c, beta=b2cc,
                  silu=True)
    ar.reset(mf2)
    dram_st(p, "l0wg", g_dram, gout0, dz="l0rg")
    ar.reset(m0)

    attF = ar.alloc([128, DT, T], BF16)
    mg = ar.mark()
    identg = load_w(p, ar, w["identity"], [128, 128])
    qb2 = ar.alloc([128, DT, T], BF16)
    kb2 = ar.alloc([128, DT, T], BF16)
    vtm2 = ar.alloc([128, 16, D], BF16)
    dram_ld(p, "l0ql", qb2, scr["q0"], dz="l0q")
    dram_ld(p, "l0kl", kb2, scr["k0"], dz="l0k")
    p.dma("l0vl", vtm2.ap.rearrange("p a b -> p (a b)"), scr["v0"],
          reads=[("d", "l0v")], writes=vtm2.z)
    probsF = [ar.alloc([128, 1280], BF16) for _ in range(2)]
    pTF = [ar.alloc([128, 9, 128], BF16) for _ in range(2)]
    rstF = [ar.alloc([128, 2], F32) for _ in range(2)]
    qsF = [ar.alloc([128, T], BF16) for _ in range(2)]
    ksF = [ar.alloc([128, T], BF16) for _ in range(2)]
    band = [ar.alloc([128, 1280], BF16) for _ in range(2)]
    it = 0
    pend3_t = [None]
    pend3_c = [None]
    pend3_a = [None, None]

    def fl3(slot):
        if slot[0] is not None:
            slot[0]()
            slot[0] = None

    for h in range(16):
        base = (h % 2) * 64
        hp = h % 2
        p.op("dve", lambda v, base=base, h=h, hp=hp: v.tensor_copy(
            qsF[hp][0:64, :], qb2.ap[base:base + 64, h // 2, :]),
            reads=qb2.z, writes=qsF[hp].z)
        p.op("dve", lambda v, base=base, h=h, hp=hp: v.tensor_copy(
            ksF[hp][0:64, :], kb2.ap[base:base + 64, h // 2, :]),
            reads=kb2.z, writes=ksF[hp].z)
        for tqt in range(16):
            pi = it % 2
            it += 1
            pr, pt, rsl, bd = probsF[pi], pTF[pi], rstF[pi], band[pi]
            bb = 0 if pi == 0 else 4
            t0 = tqt * 128
            dmax = min(t0, 1024)
            W = dmax + 128
            p0 = t0 - dmax
            src = bass.AP(scr["epad"].tensor,
                          scr["epad"].offset + h * (T * SPAD) + t0 * SPAD
                          + (1151 - dmax),
                          [[SPAD - 1, 128], [1, W]])
            p.dma("l0band" + "ab"[pi], bd.ap[:, 0:W], src,
                  reads=[("d", "ep", h)], writes=bd.z)
            ssp, zssp = ps.span(bb, 4)
            tq = slice(t0, t0 + 128)
            fl3(pend3_a)
            pend3_a[0] = pend3_a[1]
            pend3_a[1] = None
            for c in range((W + 511) // 512):
                c0 = c * 512
                c1 = min(c0 + 512, W)
                mm_group(p, [(ssp[:, c0:c1], qsF[hp].ap[0:64, tq],
                              ksF[hp].ap[0:64, p0 + c0:p0 + c1], True, True)],
                         reads=qsF[hp].z + ksF[hp].z, wzones=(zssp[c],))
            fl3(pend3_t)
            p.op("dve", lambda v, W=W, ssp=ssp, bd=bd: v.tensor_tensor(
                ssp[:, 0:W], ssp[:, 0:W], bd.ap[:, 0:W], ALU.add),
                reads=list(zssp) + bd.z, writes=list(zssp))
            p.op("dve", lambda v, W=W, ssp=ssp, rsl=rsl: v.reduce_max(
                rsl[:, 0:1], ssp[:, 0:W], axis=AX.X, negate=True),
                reads=list(zssp), writes=rsl.z)
            p.op("act", lambda a, W=W, ssp=ssp, pr=pr, rsl=rsl: a.activation(
                pr.ap[:, 0:W], ssp[:, 0:W], AF.Exp, bias=rsl[:, 0:1],
                accum_out=rsl[:, 1:2]),
                reads=list(zssp) + rsl.z, writes=pr.z + rsl.z)
            p.op("dve", lambda v, rsl=rsl: v.reciprocal(rsl[:, 1:2], rsl[:, 1:2]),
                 reads=rsl.z, writes=rsl.z)
            p.op("dve", lambda v, W=W, pr=pr, rsl=rsl: v.tensor_scalar(
                pr.ap[:, 0:W], pr.ap[:, 0:W], rsl[:, 1:2], None, ALU.mult),
                reads=pr.z + rsl.z, writes=pr.z)

            fl3(pend3_c)

            def stage3_t(pi=pi, pr=pr, W=W, bb=bb):
                npt = W // 128
                for i in range(npt):
                    bk, zbk = ps.bank(bb + i // 8)
                    bkv = bk.bitcast(BF16)[:, (i % 8) * 128:(i % 8 + 1) * 128]
                    p.op("pe", lambda t, i=i, bkv=bkv, pr=pr: t.transpose(
                        bkv, pr[:, i * 128:(i + 1) * 128], identg.ap),
                        reads=pr.z + identg.z, writes=(zbk,))

            def stage3_c(pi=pi, pt=pt, W=W, bb=bb):
                npt = W // 128
                for j in range((npt + 7) // 8):
                    nj = min(8, npt - j * 8)
                    bk, zbk = ps.bank(bb + j)
                    bkv = bk.bitcast(BF16)[:, 0:nj * 128]
                    p.op("dve", lambda v, j=j, nj=nj, bkv=bkv, pt=pt: v.tensor_copy(
                        pt.ap[:, j * 8:j * 8 + nj, :].rearrange("p a b -> p (a b)"),
                        bkv),
                        reads=[zbk], writes=pt.z)

            def stage3_a(pi=pi, pt=pt, h=h, base=base, tq=tq, W=W, bb=bb,
                         p0=p0):
                npt = W // 128
                bk6, zbk6 = ps.bank(bb + 2, rows=64, cols=128)
                mm_group(p, [(bk6, vtm2[:, p0 // 128 + i, h * 64:(h + 1) * 64],
                              pt[:, i, :], i == 0, i == npt - 1)
                             for i in range(npt)],
                         reads=vtm2.z + pt.z, wzones=(zbk6,))
                p.op("act", lambda a, base=base, h=h, tq=tq, bk6=bk6: a.activation(
                    attF.ap[base:base + 64, h // 2, tq], bk6, AF.Identity),
                    reads=[zbk6], writes=attF.z)
            pend3_t[0] = stage3_t
            pend3_c[0] = stage3_c
            pend3_a[1] = stage3_a
    fl3(pend3_a)
    fl3(pend3_t)
    fl3(pend3_c)
    pend3_a[0] = pend3_a[1]
    fl3(pend3_a)
    ar.reset(mg)
    wpr = load_w(p, ar, w["L0_attn_Wproj"], [128, DT, D])
    bprc = load_f32(p, ar, w["l0_bprojcol"], [128, DT])
    youtF = ar.alloc([128, DT, T], BF16)

    def ev_pr(a, bk, o, m):
        return a.activation(o, bk, AF.Identity, bias=bprc[:, m:m + 1])
    proj(p, ps, wpr, attF, youtF, DT, ev_pr, bprc.z, bank0=0)
    dram_st(p, "l0ws", s_dram, youtF, dz="l0rs")
    ar.reset(m0)
    phase_residual(p, ps, ar, "l0r", x_dram, s_dram, g_dram)


def prep_l0(inp):
    w = {}
    w["l0_WqT"] = lhsT_host(np.asarray(inp["L0_attn_Wq"]))
    w["l0_WkT"] = lhsT_host(np.asarray(inp["L0_attn_Wk"]))
    w["L0_attn_Wv"] = lhsT_host(np.asarray(inp["L0_attn_Wv"]))
    w["L0_attn_Wproj"] = lhsT_host(np.asarray(inp["L0_attn_Wproj"]))
    w["l0_bqcol"] = col_host(np.asarray(inp["L0_attn_bq"]))
    w["l0_bprojcol"] = col_host(np.asarray(inp["L0_attn_bproj"]))
    Wr = np.asarray(inp["L0_attn_Wr"])
    Wrp = np.zeros((1024, 1024), np.float32)
    brp = np.zeros(1024, np.float32)
    br = np.asarray(inp["L0_attn_br"])
    for h in range(16):
        Wrp[:, h * 64:h * 64 + 10] = Wr[:, h * 10:(h + 1) * 10]
        brp[h * 64:h * 64 + 10] = br[h * 10:(h + 1) * 10]
    w["l0_WrT"] = lhsT_host(Wrp)
    w["l0_brcol"] = col_host(brp)
    w["l0_bndrev"] = np.ascontiguousarray(np.asarray(inp["L0_attn_b_nd"])[:, ::-1])
    cm = np.where(np.arange(128)[:, None] >= np.arange(128)[None, :], 0.0,
                  NEG).astype(np.float32)
    w["l0_causal"] = cm
    w["L0_ffn_Wdkv"] = lhsT_host(np.asarray(inp["L0_ffn_Wdkv"]))
    w["l0_bdkvcol"] = col_host(np.asarray(inp["L0_ffn_bdkv"]))
    w["L0_ffn_Wdq"] = lhsT_host(np.asarray(inp["L0_ffn_Wdq"]))
    w["l0_bdqcol"] = col_host(np.asarray(inp["L0_ffn_bdq"]))
    w["L0_ffn_Wuk"] = lhsT_host(np.asarray(inp["L0_ffn_Wuk"]))
    w["l0_bukcol"] = col_host(np.asarray(inp["L0_ffn_buk"]))
    w["L0_ffn_Wuq"] = lhsT_host(np.asarray(inp["L0_ffn_Wuq"]))
    w["l0_buqcol"] = col_host(np.asarray(inp["L0_ffn_buq"]))
    w["L0_ffn_Wuv"] = lhsT_host(np.asarray(inp["L0_ffn_Wuv"]))
    w["l0_buvrow"] = np.ascontiguousarray(
        np.asarray(inp["L0_ffn_buv"]).reshape(1, 1024))
    w["L0_ffn_Wo"] = lhsT_host(np.asarray(inp["L0_ffn_Wo"]))
    w["l0_bocol"] = col_host(np.asarray(inp["L0_ffn_bo"]))
    w["l0_gamma1col"] = col_host(np.asarray(inp["L0_ffn_gamma1"]))
    w["l0_beta1col"] = col_host(np.asarray(inp["L0_ffn_beta1"]))
    w["l0_gamma2col"] = col_host(np.asarray(inp["L0_ffn_gamma2"]))
    w["l0_beta2col"] = col_host(np.asarray(inp["L0_ffn_beta2"]))
    wgp = np.zeros((1024, 128), np.float32)
    wgp[:, 0:8] = np.asarray(inp["L0_ffn_Wg"])
    w["l0_WgpadT"] = lhsT_host(wgp)
    wnp_ = np.zeros((1024, 128), np.float32)
    wnp_[:, 0:8] = np.asarray(inp["L0_ffn_Wn"])
    w["l0_WnpadT"] = lhsT_host(wnp_)
    bgp = np.zeros((128, 1), np.float32)
    bgp[0:8, 0] = np.asarray(inp["L0_ffn_bg"])
    w["l0_bgpadcol"] = bgp
    bnp_ = np.zeros((128, 1), np.float32)
    bnp_[0:8, 0] = np.asarray(inp["L0_ffn_bn"])
    w["l0_bnpadcol"] = bnp_
    w1 = np.asarray(inp["L0_ffn_W1"])
    w["l0_w1"] = np.ascontiguousarray(
        np.stack([w1[e].reshape(8, 128, 8, 512).transpose(2, 1, 0, 3)
                  for e in range(8)]))
    w2 = np.asarray(inp["L0_ffn_W2"])
    w["l0_w2"] = np.ascontiguousarray(
        np.stack([w2[e].reshape(8, 4, 128, 1024).transpose(0, 2, 1, 3)
                  for e in range(8)]))
    w["l0_b1cols"] = np.ascontiguousarray(
        np.stack([col_host(np.asarray(inp["L0_ffn_b1"][e])) for e in range(8)]))
    w["L0_ffn_b2"] = np.asarray(inp["L0_ffn_b2"])
    w["identity"] = np.eye(128, dtype=np.float32)
    return w


L0_WSHAPES = dict(
    l0_WqT=[128, DT, D], l0_WkT=[128, DT, D], L0_attn_Wv=[128, DT, D],
    L0_attn_Wproj=[128, DT, D], l0_bqcol=[128, DT], l0_bprojcol=[128, DT],
    l0_WrT=[128, DT, D], l0_brcol=[128, DT], l0_bndrev=[10, D],
    l0_causal=[128, 128],
    L0_ffn_Wdkv=[128, DT, 512], l0_bdkvcol=[128, 4],
    L0_ffn_Wdq=[128, DT, 512], l0_bdqcol=[128, 4],
    L0_ffn_Wuk=[128, 4, D], l0_bukcol=[128, DT],
    L0_ffn_Wuq=[128, 4, D], l0_buqcol=[128, DT],
    L0_ffn_Wuv=[128, 4, D], l0_buvrow=[1, D],
    L0_ffn_Wo=[128, DT, D], l0_bocol=[128, DT],
    l0_gamma1col=[128, DT], l0_beta1col=[128, DT],
    l0_gamma2col=[128, DT], l0_beta2col=[128, DT],
    l0_WgpadT=[128, DT, 128], l0_WnpadT=[128, DT, 128],
    l0_bgpadcol=[128, 1], l0_bnpadcol=[128, 1],
    l0_w1=[8, 8, 128, 8, 512], l0_w2=[8, 8, 128, 4, 1024],
    l0_b1cols=[8, 128, 32], L0_ffn_b2=[8, 1024],
    identity=[128, 128],
)


def l0_scratch(nc):
    return dict(
        xnf=nc.dram_tensor("l0_xnf", [D, T], F32, kind="Internal").ap(),
        q0=nc.dram_tensor("l0_q0", [D, T], BF16, kind="Internal").ap(),
        k0=nc.dram_tensor("l0_k0", [D, T], BF16, kind="Internal").ap(),
        v0=nc.dram_tensor("l0_v0", [128, 16 * D], BF16, kind="Internal").ap(),
        epad=nc.dram_tensor("l0_epad", [16, T, SPAD], BF16, kind="Internal").ap(),
        att2=nc.dram_tensor("l0_att2", [D, T], BF16, kind="Internal").ap(),
        y2=nc.dram_tensor("l0_y2", [D, T], F32, kind="Internal").ap(),
        wrt0=nc.dram_tensor("l0_wrt0", [8, T], BF16, kind="Internal").ap(),
    )


def build_layer0_test():
    nc = bass.Bass("TRN2", target_bir_lowering=False, debug=False)
    x = nc.dram_tensor("x", [D, T], F32, kind="ExternalInput").ap()
    noise = nc.dram_tensor("noise", [8, T], F32, kind="ExternalInput").ap()
    w = {k: nc.dram_tensor(k, shp, F32, kind="ExternalInput").ap()
         for k, shp in L0_WSHAPES.items()}
    xout = nc.dram_tensor("xout", [D, T], F32, kind="ExternalOutput").ap()
    s_d = nc.dram_tensor("s_scr", [D, T], BF16, kind="ExternalOutput").ap()
    g_d = nc.dram_tensor("g_scr", [D, T], BF16, kind="ExternalOutput").ap()
    xi = nc.dram_tensor("x_int", [D, T], F32, kind="Internal").ap()
    p = Prog(nc)
    ps = Psum(nc)
    ar = Arena(nc, kb=202)
    scr = l0_scratch(nc)
    tmp = ar.alloc([128, DT, T], F32)
    for d in range(DT):
        p.dma("xc", tmp[:, d, :], x[d * 128:(d + 1) * 128, :],
              writes=tmp.zt(d, DT))
        p.dma("xc2", xi[d * 128:(d + 1) * 128, :], tmp[:, d, :],
              reads=tmp.zt(d, DT), writes=[("d", "x", d)])
    ar.reset(0)
    layer0(p, ps, ar, w, xi, s_d, g_d, scr, noise)
    tmp2 = ar.alloc([128, DT, T], F32)
    for d in range(DT):
        p.dma("xo", tmp2[:, d, :], xi[d * 128:(d + 1) * 128, :],
              reads=[("d", "x", d)], writes=tmp2.zt(d, DT))
        p.dma("xo2", xout[d * 128:(d + 1) * 128, :], tmp2[:, d, :],
              reads=tmp2.zt(d, DT))
    p.emit()
    return nc


def phase_precast(p, ps, ar, nm, srcs):
    m0 = ar.mark()
    bufs = [ar.alloc([128, 4096], BF16) for _ in range(2)]
    k = 0
    for (src, dst, wztag) in srcs:
        n0, n1 = src.shape[0], src.shape[1]
        for a_ in range(n0):
            for b_ in range(n1):
                pi = k % 2
                k += 1
                shp = src[a_, b_].shape
                va = bufs[pi].ap.rearrange("p (a b) -> p a b", a=shp[1])
                p.dma(nm + "pc" + "ab"[pi], va, src[a_, b_],
                      writes=bufs[pi].z, queue="gp")
                p.dma(nm + "ps" + "ab"[pi], dst[a_, b_], va,
                      reads=bufs[pi].z, writes=[("pc", wztag, a_, b_)],
                      queue="sp")
    ar.reset(m0)



ALL_WSHAPES = {}
ALL_WSHAPES.update(L0_WSHAPES)
ALL_WSHAPES.update(L1_WSHAPES)
ALL_WSHAPES.update(L2_WSHAPES)
ALL_WSHAPES.update(L3_WSHAPES)

LAYER_WKEYS = {0: list(L0_WSHAPES), 1: list(L1_WSHAPES),
               2: list(L2_WSHAPES), 3: list(L3_WSHAPES)}

GROUPS = [[0, 1, 2, 3]]


def build_group(layers):
    nc = bass.Bass("TRN2", target_bir_lowering=False, debug=False)
    x = nc.dram_tensor("x", [D, T], F32, kind="ExternalInput").ap()
    keys = sorted({k for l_ in layers for k in LAYER_WKEYS[l_]})
    w = {k: nc.dram_tensor(k, ALL_WSHAPES[k], F32, kind="ExternalInput").ap()
         for k in keys}
    noise = (nc.dram_tensor("noise", [8, T], F32, kind="ExternalInput").ap()
             if 0 in layers else None)
    xout = nc.dram_tensor("xout", [D, T], F32, kind="ExternalOutput").ap()
    xi = nc.dram_tensor("x_int", [D, T], F32, kind="Internal").ap()
    p = Prog(nc)
    ps = Psum(nc)
    ar = Arena(nc, kb=202)
    tmp = ar.alloc([128, DT, T], F32)
    for d in range(DT):
        p.dma("xc", tmp[:, d, :], x[d * 128:(d + 1) * 128, :],
              writes=tmp.zt(d, DT))
        p.dma("xc2", xi[d * 128:(d + 1) * 128, :], tmp[:, d, :],
              reads=tmp.zt(d, DT), writes=[("d", "x", d)])
    ar.reset(0)
    mirrors = {}
    pcs = []
    if 0 in layers:
        mirrors["l0w1"] = nc.dram_tensor("m_l0w1", [8, 8, 128, 8, 512], BF16,
                                         kind="Internal").ap()
        mirrors["l0w2"] = nc.dram_tensor("m_l0w2", [8, 8, 128, 4, 1024], BF16,
                                         kind="Internal").ap()
        pcs += [(w["l0_w1"], mirrors["l0w1"], "l0w1"),
                (w["l0_w2"], mirrors["l0w2"], "l0w2")]
    if 3 in layers:
        mirrors["l3w1"] = nc.dram_tensor("m_l3w1", [8, 8, 128, 8, 512], BF16,
                                         kind="Internal").ap()
        mirrors["l3w2"] = nc.dram_tensor("m_l3w2", [8, 8, 128, 4, 1024], BF16,
                                         kind="Internal").ap()
        pcs += [(w["l3_w1"], mirrors["l3w1"], "l3w1"),
                (w["l3_w2"], mirrors["l3w2"], "l3w2")]
    if pcs:
        p._pending_precast = lambda: phase_precast(p, ps, ar, "pc", pcs)
    for mk_, mv_ in mirrors.items():
        w["__m_" + mk_] = mv_
    for l_ in layers:
        if l_ == 0:
            scr = l0_scratch(nc)
            layer0(p, ps, ar, w, xi, 
                   nc.dram_tensor("l0_s", [D, T], BF16, kind="Internal").ap(),
                   nc.dram_tensor("l0_g", [D, T], BF16, kind="Internal").ap(),
                   scr, noise)
        elif l_ == 1:
            layer1(p, ps, ar, w, xi,
                   nc.dram_tensor("l1_s", [D, T], BF16, kind="Internal").ap(),
                   nc.dram_tensor("l1_g", [D, T], BF16, kind="Internal").ap())
        elif l_ == 2:
            layer2(p, ps, ar, w, xi,
                   nc.dram_tensor("l2_s", [D, T], BF16, kind="Internal").ap(),
                   nc.dram_tensor("l2_g", [D, T], BF16, kind="Internal").ap(),
                   nc.dram_tensor("l2_zmoe", [D, T], F32, kind="Internal").ap(),
                   nc.dram_tensor("l2_wrt", [8, T], BF16, kind="Internal").ap())
        else:
            layer3(p, ps, ar, w, xi,
                   nc.dram_tensor("l3_s", [D, T], BF16, kind="Internal").ap(),
                   nc.dram_tensor("l3_g", [D, T], BF16, kind="Internal").ap(),
                   nc.dram_tensor("l3_wrt", [8, T], BF16, kind="Internal").ap())
        ar.reset(0)
    tmp2 = ar.alloc([128, DT, T], F32)
    for d in range(DT):
        p.dma("xo", tmp2[:, d, :], xi[d * 128:(d + 1) * 128, :],
              reads=[("d", "x", d)], writes=tmp2.zt(d, DT))
        p.dma("xo2", xout[d * 128:(d + 1) * 128, :], tmp2[:, d, :],
              reads=tmp2.zt(d, DT))
    ar.reset(0)
    p.emit()
    return nc, keys


def make_noise():
    M = np.uint64(0xFFFFFFFF)
    ROT = [[13, 15, 26, 6], [17, 29, 16, 24]]

    def rotl(x, d_):
        return ((x << np.uint64(d_)) | (x >> np.uint64(32 - d_))) & M

    n_el = 8 * 2048 * 8
    x0 = np.zeros(n_el, np.uint64)
    x1 = np.arange(n_el, dtype=np.uint64)
    k = [np.uint64(0), np.uint64(42)]
    k.append((k[0] ^ k[1] ^ np.uint64(0x1BD11BDA)) & M)
    x0 = (x0 + k[0]) & M
    x1 = (x1 + k[1]) & M
    ks = [k[1], k[2], k[0]]
    rots = ROT[:]
    for i in range(5):
        for r in rots[0]:
            x0 = (x0 + x1) & M
            x1 = rotl(x1, r)
            x1 = (x1 ^ x0) & M
        x0 = (x0 + ks[0]) & M
        x1 = (x1 + ks[1] + np.uint64(i + 1)) & M
        ks = ks[1:] + ks[:1]
        rots = rots[1:] + rots[:1]
    bits = x0.astype(np.uint32) ^ x1.astype(np.uint32)
    fb = ((bits >> np.uint32(9)) | np.float32(1.0).view(np.uint32)).view(np.float32)
    f = fb - np.float32(1.0)
    lo = np.float32(np.nextafter(np.float32(-1.0), np.float32(0.0)))
    hi = np.float32(1.0)
    u = np.maximum(lo, (f * (hi - lo) + lo).astype(np.float32))
    try:
        from scipy.special import erfinv
        vals = (np.float32(np.sqrt(2.0))
                * erfinv(u.astype(np.float64)).astype(np.float32))
    except Exception:
        x = u.astype(np.float64)
        w = -np.log((1.0 - x) * (1.0 + x))
        p = np.where(w < 5.0,
                     2.81022636e-08, -2.00214257e-04)
        w1 = np.where(w < 5.0, w - 2.5, np.sqrt(w) - 3.0)
        c_lo = [2.81022636e-08, 3.43273939e-07, -3.5233877e-06,
                -4.39150654e-06, 0.00021858087, -0.00125372503,
                -0.00417768164, 0.246640727, 1.50140941]
        c_hi = [-0.000200214257, 0.000100950558, 0.00134934322,
                -0.00367342844, 0.00573950773, -0.0076224613,
                0.00943887047, 1.00167406, 2.83297682]
        p_lo = np.zeros_like(w1)
        p_hi = np.zeros_like(w1)
        for c_ in c_lo:
            p_lo = p_lo * w1 + c_
        for c_ in c_hi:
            p_hi = p_hi * w1 + c_
        p = np.where(w < 5.0, p_lo, p_hi)
        vals = (np.sqrt(2.0) * p * x).astype(np.float32)
    return vals.reshape(8, 2048, 8)


def kernel(**inputs):
    inp = {k: np.asarray(v) for k, v in inputs.items()}
    w = {}
    w.update(prep_l0(inp))
    w.update(prep_l1(inp))
    w.update(prep_l2(inp))
    w.update(prep_l3(inp))
    noise = make_noise()
    X = np.asarray(inp["L0_attn_X"], np.float32)
    xTs = [np.ascontiguousarray(X[b_].T) for b_ in range(B)]
    noiseTs = [np.ascontiguousarray(noise[b_].T) for b_ in range(B)]
    for layers in GROUPS:
        nc, keys = build_group(layers)
        in_maps = []
        for b_ in range(B):
            m_ = {k: w[k] for k in keys}
            m_["x"] = xTs[b_]
            if 0 in layers:
                m_["noise"] = noiseTs[b_]
            in_maps.append(m_)
        res = run_bass_kernel_spmd(nc, in_maps, core_ids=list(range(NCORES)))
        xTs = [np.asarray(res.results[b_]["xout"]) for b_ in range(B)]
    out = np.stack([xT.T for xT in xTs])
    return np.ascontiguousarray(out.astype(np.float32))
